# version 13
# baseline (speedup 1.0000x reference)
import numpy as np
import ml_dtypes
import concourse.bass as bass
import concourse.mybir as mybir
from concourse.bass_utils import run_bass_kernel_spmd

F32 = mybir.dt.float32
BF16 = mybir.dt.bfloat16
AF = mybir.ActivationFunctionType
ALU = mybir.AluOpType
AX = mybir.AxisListType

D = 1024
NCH = 8
T = 2048
TT = 512
NTT = 4
NG = 16
R = 16
NE = 8
FH = 2048
FHC = 16
MH = 4096
MHC = 32
EPS_RMS = 1e-6
EPS_LN = 1e-5
EPS_L2 = 1e-8

DEBUG_TAPS = ()
_NC_CACHE = {}


class Sem:
    def __init__(self, nc, name):
        self.h = nc.alloc_semaphore(name)
        self.n = 0


def _flat(ws):
    out = []
    for w in ws:
        if w is None:
            continue
        if isinstance(w, list):
            out.extend(_flat(w))
        else:
            out.append(w)
    return out


class B:
    ENGS = ("tensor", "vector", "scalar", "sync", "gpsimd")

    def __init__(self, taps=()):
        nc = bass.Bass("TRN2", target_bir_lowering=False)
        self.nc = nc
        self.em = {n: [] for n in self.ENGS}
        self.sat = {n: {} for n in self.ENGS}
        self.sT = Sem(nc, "sT")
        self.sV = Sem(nc, "sV")
        self.sA = Sem(nc, "sA")
        self.sG = Sem(nc, "sG")
        self.dC = Sem(nc, "dC")
        self.dHR = [Sem(nc, "dHR0"), Sem(nc, "dHR1")]
        self.dHB = [Sem(nc, "dHB0"), Sem(nc, "dHB1")]
        self.dHF = [Sem(nc, "dHF0"), Sem(nc, "dHF1")]
        self.dWS = [Sem(nc, "dWS0"), Sem(nc, "dWS1")]
        self.dW2 = [Sem(nc, "dW20"), Sem(nc, "dW21")]
        self.dWD = [Sem(nc, "dWD0"), Sem(nc, "dWD1")]
        self.dAs = [Sem(nc, "dA0"), Sem(nc, "dA1")]
        self.dGs = [Sem(nc, "dG0"), Sem(nc, "dG1")]
        self.dO = Sem(nc, "dO")
        self.taps = {}
        self.tap_names = list(taps)
        self.bank_r = [None] * 8
        self.bank_w = [None] * 8
        self.gtmp_free = [None, None]
        self.xnf_free = [None, None]
        self.hn_free = None
        self.mid_free = None
        self.ara_free = None
        self.row_free = None
        self.fmid_free = None
        self.mkb_free = None
        self.scsb_free = None
        self.sctm_free = None
        self.msk_free = None
        self.hftr_free = [None, None]
        self.qke_free = None
        self.wd_slot = [None, None]
        self.ws_slot = [None, None]
        self._declare_dram()
        self._declare_consts()
        self._declare_sbuf()

    def op(self, eng, meth, *a, wait=(), inc=None, incv=1, **kw):
        waits = []
        for (s, v) in _flat(list(wait)):
            if self.sat[eng].get(s, -1) < v:
                self.sat[eng][s] = v
                waits.append((s.h, v))
        if inc is None:
            inc = {"tensor": self.sT, "vector": self.sV, "scalar": self.sA,
                   "gpsimd": self.sG}[eng]
        inc.n += incv
        ticket = (inc.h, incv)
        ret = (inc, inc.n)

        def f(e, meth=meth, a=a, kw=kw, waits=waits, ticket=ticket):
            for (h, v) in waits:
                e.wait_ge(h, v)
            getattr(e, meth)(*a, **kw).then_inc(*ticket)
        self.em[eng].append(f)
        return ret

    def waitv(self, eng, *tickets):
        for (s, v) in _flat(list(tickets)):
            if self.sat[eng].get(s, -1) < v:
                self.sat[eng][s] = v
                self.em[eng].append(lambda e, h=s.h, v=v: e.wait_ge(h, v))

    def dma(self, q, sem, out, in_, wait=(), slow=False):
        kw = {"allow_slow_non_contiguous": True} if slow else {}
        return self.op(q, "dma_start", wait=wait, inc=sem, incv=16,
                       out=out, in_=in_, **kw)

    def mm(self, bank_ids, meth, out, *a, wait=(), start=None, stop=None,
           first=False, last=False, **kw):
        w = list(wait)
        if first:
            for b_ in bank_ids:
                w.append(self.bank_r[b_])
        if meth == "matmul":
            t = self.op("tensor", "matmul", out, *a, wait=w,
                        start=start, stop=stop, **kw)
        else:
            t = self.op("tensor", meth, out, *a, wait=w, **kw)
        if last:
            for b_ in bank_ids:
                self.bank_w[b_] = t
        return t

    def bank_read(self, eng, meth, bank_ids, *a, wait=(), **kw):
        w = list(wait) + [self.bank_w[b_] for b_ in bank_ids]
        t = self.op(eng, meth, *a, wait=w, **kw)
        for b_ in bank_ids:
            self.bank_r[b_] = t
        return t

    def _declare_dram(self):
        nc = self.nc

        def inp(name, shape):
            return nc.dram_tensor(name, list(shape), F32, kind="ExternalInput").ap()
        self.h_fm = inp("h_fm", (D, T))
        self.h_tm = inp("h_tm", (T, D))
        self.hp_fm = inp("hp_fm", (D, T))
        self.hp_tm = inp("hp_tm", (T, D))
        self.uv = inp("uv", (D, 2 * R))
        self.projT = inp("projT", (D, D))
        self.w1h = inp("w1h", (MHC, 128, 8 * 128))
        self.w2w = inp("w2w", (MH, D))
        self.upb = inp("upb", (NE, FHC, 128, 8 * 256))
        self.dwn = inp("dwn", (NE, FH, D))
        self.cenT = inp("cenT", (D, NE))
        self.n1w = inp("n1w", (D,))
        self.n2w = inp("n2w", (D,))
        self.fsc = inp("fsc", (D,))
        self.pbv = inp("pbv", (D,))
        self.b1v = inp("b1v", (MH,))
        self.b2v = inp("b2v", (D,))
        self.dlog = inp("dlog", (1, R))
        self.alog = inp("alog", (1, R))
        self.glog = inp("glog", (1, 1))
        self.kker = inp("kker", (1, 4))
        self.w1bc = inp("w1bc", (128, D))
        self.y_out = nc.dram_tensor("y_out", [D, T], F32, kind="ExternalOutput").ap()

        def scr(name, shape):
            return nc.dram_tensor(name, list(shape), BF16, kind="Internal").ap()
        self.sc_proj = [scr("sc_proj_h", (NCH, 128, D)), scr("sc_proj_l", (NCH, 128, D))]
        self.sc_w1 = [scr("sc_w1_h", (MHC, 128, 8 * 128)),
                      scr("sc_w1_l", (MHC, 128, 8 * 128))]
        self.sc_w2 = [scr("sc_w2_h", (MHC, 128, D)), scr("sc_w2_l", (MHC, 128, D))]
        self.sc_up = [scr("sc_up_h", (NE, FHC, 128, 8 * 256)),
                      scr("sc_up_l", (NE, FHC, 128, 8 * 256))]
        self.sc_dw = [scr("sc_dw_h", (NE, FHC, 128, D)),
                      scr("sc_dw_l", (NE, FHC, 128, D))]

    def _declare_consts(self):
        nc = self.nc

        def ic(name, arr):
            return nc.inline_tensor(np.ascontiguousarray(arr), name=name).ap()
        self.IDT_I = ic("c_idt", np.eye(128, dtype=np.float32))
        self.ONESR_I = ic("c_onesr", np.ones((1, 128), np.float32))
        self.ONESC_I = ic("c_onesc", np.ones((128, 1), np.float32))
        self.TRIU_I = ic("c_triu", np.triu(np.ones((128, 128), np.float32)))
        dt_np = np.zeros((128, 4 * 128), np.float32)
        for tau in range(4):
            for j in range(128 - tau):
                dt_np[j, tau * 128 + j + tau] = 1.0
        self.DTAU_I = ic("c_dtau", dt_np)
        pt_np = np.zeros((3, 3 * 128), np.float32)
        for tau in range(1, 4):
            for s in range(3):
                i = tau + s - 3
                if 0 <= i < 128:
                    pt_np[s, (tau - 1) * 128 + i] = 1.0
        self.PTAU_I = ic("c_ptau", pt_np)
        self.GQ_I = ic("c_gq", np.broadcast_to(
            (np.arange(128) % 32).astype(np.float32), (16, 128)))
        self.GQ2_I = ic("c_gq2", np.broadcast_to(
            (np.arange(128) + 1).astype(np.float32), (16, 128)))
        self.GKB_I = ic("c_gkb", np.broadcast_to(
            (127 - np.arange(128)).astype(np.float32)[:, None], (128, 16)))
        be_np = np.zeros((8, 8 * 128), ml_dtypes.bfloat16)
        for e in range(8):
            be_np[e, e * 128:(e + 1) * 128] = 1.0
        self.BE_I = ic("c_be", be_np)

    def _declare_sbuf(self):
        nc = self.nc
        sb = nc.alloc_sbuf_tensor
        self.AR_A = sb("AR_A", [128, NCH * TT], F32).ap()
        self.ATN = sb("ATN", [128, 10240], F32).ap()
        self.AR_B = sb("AR_B", [128, 16384], BF16).ap()
        self.AR_W = sb("AR_W", [128, 16384], BF16).ap()
        self.STG = sb("STG", [128, 2048], F32).ap()
        self.HNB = sb("HNB", [128, 8192], BF16).ap()
        self.HFT = sb("HFT", [128, 1024], F32).ap()
        self.GEN = sb("GEN", [128, 6144], F32).ap()
        self.S_SB = sb("S_SB", [16, 1024], F32).ap()
        self.CST = sb("CST", [128, 2048], F32).ap()
        self.CSTB = sb("CSTB", [128, 2048], BF16).ap()
        self.XTR = sb("XTR", [128, 3072], F32).ap()
        self.pbk = [nc.alloc_psum_tensor("pb%d" % i, [128, 512], F32).ap()
                    for i in range(8)]
        C = self.CST
        self.C_IDT = C[:, 0:128]
        self.C_TRIU = C[:, 128:256]
        self.C_CONV = C[:, 256:384]
        self.C_W1BC = C[:, 384:1408]
        self.C_UVW = C[:, 1408:1664]
        self.C_CT = C[:, 1664:1728]
        self.N2W = C[:, 1728:1736]
        self.FSCW = C[:, 1736:1744]
        self.PB_ = C[:, 1744:1752]
        self.B2_ = C[:, 1752:1760]
        self.B1_ = C[:, 1760:1792]
        self.N1W = C[:, 1792:1800]
        self.C_BND = C[0:3, 1800:1928]
        self.C_FKB = C[:, 1928:1944]
        self.C_GKB = C[:, 1944:1960]
        self.BE_B = self.CSTB[0:8, 0:1024]
        self.MSK_B = self.CSTB[0:8, 1024:1536]
        self.MKB_B = self.CSTB[:, 1536:2048]
        G = self.GEN
        self.G_XNF = [G[:, 0:512], G[:, 512:1024]]
        self.G_TMP = [G[:, 1024:1536], G[:, 1536:2048]]
        self.F_Q0 = G[0:16, 2048:2176]
        self.F_KT = G[0:16, 2176:2304]
        self.F_Q2 = G[0:16, 2304:2432]
        self.F_SCR = G[0:16, 2432:2560]
        self.LNG_C = G[0:16, 2560:2561]
        self.ALP_C = G[0:16, 2561:2562]
        self.G32_C = G[0:16, 2562:2563]
        self.G128_C = G[0:16, 2563:2564]
        self.GROW = G[0:1, 2564:2580]
        self.LNGROW = G[0:1, 2580:2596]
        self.AROW = G[0:1, 2596:2612]
        self.GK = G[0:1, 2612:2616]
        self.GKB_B = G[:, 2616:2620]
        self.CSQ = G[0:8, 2620:2621]
        self.CSQ_R = G[0:1, 2624:2632]
        self.SIGG = G[0:1, 2632:2633]
        self.ONES_R = G[0:1, 2640:2768]
        self.ONES_C = G[:, 2768:2769]
        self.DLOG_S = G[0:1, 2769:2785]
        self.ALOG_S = G[0:1, 2785:2801]
        self.GLOG_S = G[0:1, 2801:2802]
        self.KKER_S = G[0:1, 2802:2806]
        self.B_HB = [G[0:3, 2816 + i * 1024:2816 + (i + 1) * 1024] for i in range(2)]
        self.B_HBST = G[0:3, 4864:4868]
        self.HBSQ = G[0:3, 4872:5896]
        X = self.XTR
        self.MROW = X[0:1, 0:512]
        self.VROW = X[0:1, 512:1024]
        self.IROW = X[0:1, 1024:1536]
        self.MIROW = X[0:1, 1536:2048]
        self.SQROW = X[0:1, 2048:2560]
        self.SC_SB = X[0:8, 2560:3072]
        self.SCTM = self.GEN[:, 5896:5904]
        self.EQT = self.GEN[:, 5904:5912]
        self.MNV = self.GEN[:, 5912:5913]
        self.NYT = self.GEN[:, 5913:5914]
        self.EXC = self.GEN[:, 5914:5922]
        Bf = self.ATN
        self.B_MIX = Bf[:, 0:4096]
        self.B_HTM = [Bf[:, 4096 + i * 1024:4096 + (i + 1) * 1024] for i in range(2)]
        self.B_HRAW = [Bf[:, 6144 + i * 1024:6144 + (i + 1) * 1024] for i in range(2)]
        self.B_QKT = [Bf[:, 8192 + i * 32:8192 + (i + 1) * 32] for i in range(2)]
        def ring(base, w, p=128):
            return [Bf[0:p, base + i * w:base + (i + 1) * w] for i in range(2)]
        self.B_SQ = ring(8256, 32)
        self.B_SSN = ring(8320, 8)
        self.B_QNT = ring(8336, 16)
        self.B_KNT = ring(8368, 16)
        self.B_KB = ring(8400, 16)
        self.B_INVT = ring(8432, 1)
        self.B_QNR = Bf[0:16, 8448:8576]
        self.B_KNR = Bf[0:16, 8576:8704]
        self.B_QT = [Bf[0:16, 8704 + i * 128:8704 + (i + 1) * 128]
                     for i in range(4)]
        self.B_KT_ = Bf[0:16, 9216:9344]
        self.B_Q2 = ring(9344, 128, 16)
        self.B_ASB = Bf[:, 9600:9728]
        self.B_QKE = Bf[0:32, 9728:10240]
        Bb = self.AR_B
        self.M_HI = Bb[:, 0:8192]
        self.M_LO = Bb[:, 8192:16384]
        self.FMID_HI = Bb[:, 0:8192]
        self.FMID_LO = Bb[:, 8192:16384]
        Wb = self.AR_W
        self.W_STG = [self.STG[:, 0:1024], self.STG[:, 1024:2048]]
        self.W_HI = [Wb[:, i * 1024:(i + 1) * 1024] for i in range(2)]
        self.W_LO = [Wb[:, 2048 + i * 1024:2048 + (i + 1) * 1024] for i in range(2)]
        self.WS_HI = [Wb[:, 4096 + i * 2048:4096 + (i + 1) * 2048] for i in range(2)]
        self.WS_LO = [Wb[:, 8192 + i * 2048:8192 + (i + 1) * 2048] for i in range(2)]
        self.WD_HI = [Wb[:, 12288 + i * 1024:12288 + (i + 1) * 1024] for i in range(2)]
        self.WD_LO = [Wb[:, 14336 + i * 1024:14336 + (i + 1) * 1024] for i in range(2)]
        self.HN_HI = self.HNB[:, 0:4096]
        self.HN_LO = self.HNB[:, 4096:8192]
        self.HS_HI = self.HNB[:, 0:2048]
        self.HS_LO = self.HNB[:, 2048:4096]
        self.XN_HI = self.HNB[:, 0:4096]
        self.XN_LO = self.HNB[:, 4096:8192]

    def tap(self, name, src_ap, shape, wait=()):
        if name not in self.tap_names or name in self.taps:
            return
        ap = self.nc.dram_tensor("tap_" + name, list(shape), F32,
                                 kind="ExternalOutput").ap()
        self.taps[name] = self.dma("scalar", self.dO, ap, src_ap, wait=wait)

    def p_setup(self):
        s = self
        d = lambda out, in_, w=(), slow=False: s.dma("sync", s.dC, out, in_, wait=w, slow=slow)
        t_idt = d(s.C_IDT, s.IDT_I)
        d(s.C_TRIU, s.TRIU_I)
        t_onesr = d(s.ONES_R, s.ONESR_I)
        t_onesc = d(s.ONES_C, s.ONESC_I)
        d(s.C_W1BC, s.w1bc)
        t_gkb = d(s.C_GKB, s.GKB_I)
        t_gq = d(s.G_TMP[0][0:16, 0:128], s.GQ_I)
        t_gq2 = d(s.G_TMP[0][0:16, 128:256], s.GQ2_I)
        d(s.BE_B, s.BE_I)
        t_dtau = d(s.B_MIX[:, 0:512], s.DTAU_I)
        t_ptau = d(s.B_MIX[0:3, 512:896], s.PTAU_I)
        t_uv = d(s.C_UVW, s.uv.rearrange("(c p) r -> p c r", p=128))
        t_ct = d(s.C_CT, s.cenT.rearrange("(c p) e -> p c e", p=128))
        t_n1 = d(s.N1W, s.n1w.rearrange("(c p) -> p c", p=128), slow=True)
        d(s.N2W, s.n2w.rearrange("(c p) -> p c", p=128), slow=True)
        d(s.FSCW, s.fsc.rearrange("(c p) -> p c", p=128), slow=True)
        d(s.PB_, s.pbv.rearrange("(c p) -> p c", p=128), slow=True)
        d(s.B2_, s.b2v.rearrange("(c p) -> p c", p=128), slow=True)
        d(s.B1_, s.b1v.rearrange("(c p) -> p c", p=128), slow=True)
        t_dl = d(s.DLOG_S, s.dlog)
        t_al = d(s.ALOG_S, s.alog)
        t_gl = d(s.GLOG_S, s.glog)
        t_kk = d(s.KKER_S, s.kker)
        L = (s.dC, s.dC.n)

        op = s.op
        a1 = op("scalar", "activation", out=s.GROW, in_=s.DLOG_S,
                func=AF.Sigmoid, wait=[L])
        v1 = op("vector", "tensor_scalar", out=s.GROW, in0=s.GROW,
                scalar1=0.15, scalar2=1.0, op0=ALU.max, op1=ALU.min, wait=[a1])
        a2 = op("scalar", "activation", out=s.LNGROW, in_=s.GROW,
                func=AF.Ln, wait=[v1])
        a3 = op("scalar", "activation", out=s.AROW, in_=s.ALOG_S,
                func=AF.Sigmoid, wait=[L])
        a4 = op("scalar", "activation", out=s.SIGG, in_=s.GLOG_S,
                func=AF.Sigmoid, wait=[L])
        v2 = op("vector", "tensor_single_scalar", s.GK, s.KKER_S, s.SIGG,
                op=ALU.mult, wait=[a4, L])
        m1 = s.mm([3], "transpose", s.pbk[3][0:16, 0:1], s.LNGROW,
                  s.C_IDT[0:1, 0:1], wait=[a2, L], first=True, last=True)
        v3 = s.bank_read("vector", "tensor_copy", [3], s.LNG_C,
                         s.pbk[3][0:16, 0:1])
        m2 = s.mm([3], "transpose", s.pbk[3][0:16, 0:1], s.AROW,
                  s.C_IDT[0:1, 0:1], wait=[a3], first=True, last=True)
        v4 = s.bank_read("vector", "tensor_copy", [3], s.ALP_C,
                         s.pbk[3][0:16, 0:1])
        v5 = op("vector", "tensor_single_scalar", s.F_SCR,
                s.G_TMP[0][0:16, 0:128], s.LNG_C, op=ALU.mult, wait=[L, v3])
        a5 = op("scalar", "activation", out=s.F_Q0, in_=s.F_SCR, func=AF.Exp,
                wait=[v5])
        op("vector", "tensor_single_scalar", s.F_Q0, s.F_Q0, s.ALP_C,
           op=ALU.mult, wait=[a5, v4])
        v7 = op("vector", "tensor_scalar", out=s.F_SCR, in0=s.F_SCR,
                scalar1=-1.0, scalar2=None, op0=ALU.mult, wait=[v5, a5])
        a6 = op("scalar", "activation", out=s.F_KT, in_=s.F_SCR, func=AF.Exp,
                wait=[v7])
        v8 = op("vector", "tensor_single_scalar", s.F_SCR,
                s.G_TMP[0][0:16, 128:256], s.LNG_C, op=ALU.mult,
                wait=[L, v3, a6])
        a7 = op("scalar", "activation", out=s.F_Q2, in_=s.F_SCR, func=AF.Exp,
                wait=[v8])
        op("vector", "tensor_single_scalar", s.F_Q2, s.F_Q2, s.ALP_C,
           op=ALU.mult, wait=[a7, v4])
        vA = op("vector", "tensor_scalar", out=s.G32_C, in0=s.LNG_C,
                scalar1=32.0, scalar2=None, op0=ALU.mult, wait=[v3])
        op("scalar", "activation", out=s.G32_C, in_=s.G32_C, func=AF.Exp,
           wait=[vA])
        vB = op("vector", "tensor_scalar", out=s.G128_C, in0=s.LNG_C,
                scalar1=128.0, scalar2=None, op0=ALU.mult, wait=[v3])
        op("scalar", "activation", out=s.G128_C, in_=s.G128_C, func=AF.Exp,
           wait=[vB])
        mB = s.mm([3], "matmul", s.pbk[3][:, 0:16], s.ONES_R, s.LNGROW,
                  start=True, stop=True, wait=[L, a2], first=True,
                  last=True)
        vC = s.bank_read("vector", "tensor_tensor", [3], s.C_FKB, s.C_GKB,
                         s.pbk[3][:, 0:16], op=ALU.mult, wait=[L])
        op("scalar", "activation", out=s.C_FKB, in_=s.C_FKB, func=AF.Exp,
           wait=[vC])
        mC = s.mm([3], "matmul", s.pbk[3][:, 0:4], s.ONES_R, s.GK,
                  start=True, stop=True, wait=[v2], first=True, last=True)
        vD = s.bank_read("vector", "tensor_copy", [3], s.GKB_B, s.pbk[3][:, 0:4])
        vE = op("vector", "tensor_single_scalar", s.C_CONV, s.B_MIX[:, 0:128],
                s.GKB_B[:, 0:1], op=ALU.mult, wait=[vD, L])
        for tau in range(1, 4):
            vE = op("vector", "scalar_tensor_tensor", s.C_CONV,
                    s.B_MIX[:, tau * 128:(tau + 1) * 128],
                    s.GKB_B[:, tau:tau + 1], s.C_CONV,
                    op0=ALU.mult, op1=ALU.add, wait=[vE])
        vF = op("vector", "tensor_single_scalar", s.C_BND,
                s.B_MIX[0:3, 512:640], s.GKB_B[0:3, 1:2], op=ALU.mult,
                wait=[vD, L])
        for tau in range(2, 4):
            vF = op("vector", "scalar_tensor_tensor", s.C_BND,
                    s.B_MIX[0:3, 512 + (tau - 1) * 128:512 + tau * 128],
                    s.GKB_B[0:3, tau:tau + 1], s.C_BND,
                    op0=ALU.mult, op1=ALU.add, wait=[vF])
        vU = None
        for c in range(NCH):
            vU = op("vector", "tensor_single_scalar",
                    s.C_UVW[:, c * 32:(c + 1) * 32],
                    s.C_UVW[:, c * 32:(c + 1) * 32], s.N1W[:, c:c + 1],
                    op=ALU.mult, wait=[L, vU])
        aS = op("scalar", "activation", out=s.G_TMP[1][:, 0:64], in_=s.C_CT,
                func=AF.Square, wait=[L])
        mS = s.mm([3], "matmul", s.pbk[3][0:1, 0:64], s.ONES_C,
                  s.G_TMP[1][:, 0:64], start=True, stop=True,
                  wait=[aS, L], first=True, last=True)
        vS = s.bank_read("vector", "tensor_reduce", [3], s.CSQ_R,
                         s.pbk[3][0:1, 0:64].rearrange("p (c e) -> p e c", c=8),
                         axis=AX.X, op=ALU.add)
        mS2 = s.mm([3], "transpose", s.pbk[3][0:8, 64:65], s.CSQ_R,
                   s.C_IDT[0:1, 0:1], wait=[vS], first=True, last=True)
        s.bank_read("vector", "tensor_copy", [3], s.CSQ, s.pbk[3][0:8, 64:65])
        op("vector", "tensor_scalar", out=s.C_CT, in0=s.C_CT,
           scalar1=-2.0, scalar2=None, op0=ALU.mult, wait=[L, mS])
        op("vector", "memset", s.S_SB, 0.0)
        self.setupV = (s.sV, s.sV.n)
        self.setupA = (s.sA, s.sA.n)
        self.setupT = (s.sT, s.sT.n)
        self.setupW = (s.dC, s.dC.n)

    def p_split(self):
        s = self
        blocks = []
        for c in range(NCH):
            blocks.append((("proj", c), s.projT[c * 128:(c + 1) * 128, :],
                           s.sc_proj[0][c], s.sc_proj[1][c]))
        for cp in range(MHC):
            blocks.append((("w1", cp), s.w1h[cp], s.sc_w1[0][cp], s.sc_w1[1][cp]))
        for c in range(MHC):
            blocks.append((("w2", c), s.w2w[c * 128:(c + 1) * 128, :],
                           s.sc_w2[0][c], s.sc_w2[1][c]))
        for e in range(NE):
            for cp in range(FHC):
                for hh in range(2):
                    sl = slice(hh * 1024, (hh + 1) * 1024)
                    blocks.append((("up", e, cp, hh), s.upb[e, cp][:, sl],
                                   s.sc_up[0][e, cp][:, sl],
                                   s.sc_up[1][e, cp][:, sl]))
            for c in range(FHC):
                blocks.append((("dw", e, c), s.dwn[e, c * 128:(c + 1) * 128, :],
                               s.sc_dw[0][e, c], s.sc_dw[1][e, c]))
        self.split_done = {}
        slot_free = [None, None]
        tins = [None] * len(blocks)
        for i in range(min(2, len(blocks))):
            tins[i] = s.dma("gpsimd", s.dAs[i % 2], s.W_STG[i % 2],
                            blocks[i][1], wait=[])
        wfree = [None, None]
        for i, (key, srcap, dsth, dstl) in enumerate(blocks):
            r = i % 2
            thi = s.op("gpsimd", "tensor_copy", s.W_HI[r], s.W_STG[r],
                       wait=[tins[i], wfree[r]])
            tlo = s.op("gpsimd", "tensor_sub", s.W_LO[r], s.W_STG[r],
                       s.W_HI[r], wait=[thi])
            s.dma("gpsimd", s.dGs[r], dsth, s.W_HI[r], wait=[thi])
            so2 = s.dma("gpsimd", s.dGs[r], dstl, s.W_LO[r], wait=[tlo])
            wfree[r] = so2
            self.split_done[key] = so2
            if i + 2 < len(blocks):
                tins[i + 2] = s.dma("gpsimd", s.dAs[r], s.W_STG[r],
                                    blocks[i + 2][1], wait=[tlo])

    def qkraw_tile(self, tau, fm_ap):
        s = self
        fmv = fm_ap.rearrange("(c p) t -> p c t", p=128)
        last = None
        for c in range(NCH):
            r = c % 2
            t_h = s.dma("sync", s.dHF[r], s.HFT[:, r * 512:(r + 1) * 512],
                        fmv[:, c, tau * TT:(tau + 1) * TT],
                        wait=[self.hftr_free[r]])
            last = s.mm([4], "matmul", s.pbk[4][0:32, :],
                        s.C_UVW[:, c * 32:(c + 1) * 32],
                        s.HFT[:, r * 512:(r + 1) * 512],
                        start=(c == 0), stop=(c == NCH - 1),
                        wait=[t_h, self.setupV] + (
                            [self.qke_free] if c == 0 else []),
                        first=(c == 0), last=(c == NCH - 1))
            self.hftr_free[r] = last
        t_qke = s.bank_read("vector", "tensor_copy", [4], s.B_QKE,
                            s.pbk[4][0:32, :])
        return last, t_qke

    def chain_stats(self, g, tm_ap, is_prefix, t_qke):
        s = self
        r = g % 2
        op = s.op
        st = self.ast
        ch = st["ch"][r]
        t_raw = s.dma("sync", s.dHR[r], s.B_HRAW[r],
                      tm_ap[g * 128:(g + 1) * 128, :],
                      wait=[st["hraw_free"][r]])
        a_sq = op("scalar", "activation", out=s.B_HTM[r], in_=s.B_HRAW[r],
                  func=AF.Square, accum_out=s.B_SSN[r][:, 0:1],
                  wait=[t_raw, st["htm_free"][r], st["ssn_free"][r]])
        v_ms = op("vector", "tensor_scalar", out=s.B_SSN[r][:, 1:2],
                  in0=s.B_SSN[r][:, 0:1], scalar1=1.0 / D, scalar2=EPS_RMS,
                  op0=ALU.mult, op1=ALU.add, wait=[a_sq])
        a_sr = op("scalar", "activation", out=s.B_SSN[r][:, 2:3],
                  in_=s.B_SSN[r][:, 1:2], func=AF.Sqrt, wait=[v_ms])
        v_iv = op("vector", "reciprocal", s.B_INVT[r], s.B_SSN[r][:, 2:3],
                  wait=[a_sr, st["invt_free"][r]])
        v_h1 = op("vector", "tensor_single_scalar", s.B_HTM[r], s.B_HRAW[r],
                  s.B_INVT[r], op=ALU.mult, wait=[v_iv, a_sq])
        v_h2 = op("vector", "tensor_tensor", s.B_HTM[r], s.B_HTM[r],
                  s.C_W1BC, op=ALU.mult, wait=[v_h1])
        st["hraw_free"][r] = v_h2
        m_tr = s.mm([3], "transpose", s.pbk[3][:, 0:32],
                    s.B_QKE[0:32, (g % 4) * 128:(g % 4) * 128 + 128],
                    s.C_IDT[0:32, 0:32], wait=[t_qke, st["qkt_free"][r]],
                    first=True, last=True)
        v_qs = s.bank_read("vector", "tensor_single_scalar", [3], s.B_QKT[r],
                           s.pbk[3][:, 0:32], s.B_INVT[r], op=ALU.mult,
                           wait=[v_iv])
        st["invt_free"][r] = v_qs
        a_q2 = op("scalar", "activation", out=s.B_SQ[r], in_=s.B_QKT[r],
                  func=AF.Square, wait=[v_qs, st["sq_free"][r]])
        v_r1 = op("vector", "reduce_sum", s.B_SSN[r][:, 4:5],
                  s.B_SQ[r][:, 0:16], axis=AX.X, wait=[a_q2])
        v_r2 = op("vector", "reduce_sum", s.B_SSN[r][:, 5:6],
                  s.B_SQ[r][:, 16:32], axis=AX.X, wait=[a_q2])
        st["sq_free"][r] = v_r2
        a_n = op("scalar", "activation", out=s.B_SSN[r][:, 6:8],
                 in_=s.B_SSN[r][:, 4:6], func=AF.Sqrt, wait=[v_r1, v_r2])
        v_mx = op("vector", "tensor_scalar", out=s.B_SSN[r][:, 6:8],
                  in0=s.B_SSN[r][:, 6:8], scalar1=EPS_L2, scalar2=None,
                  op0=ALU.max, wait=[a_n])
        v_rc = op("vector", "reciprocal", s.B_SSN[r][:, 6:8],
                  s.B_SSN[r][:, 6:8], wait=[v_mx])
        v_kn = op("vector", "tensor_single_scalar", s.B_KNT[r],
                  s.B_QKT[r][:, 16:32], s.B_SSN[r][:, 7:8], op=ALU.mult,
                  wait=[v_rc, st["knt_free"][r]])
        st["qkt_free"][r] = v_kn
        st["ssn_free"][r] = v_kn
        v_kb = op("vector", "tensor_tensor", s.B_KB[r], s.B_KNT[r], s.C_FKB,
                  op=ALU.mult, wait=[v_kn, self.setupA, st["kb_free"][r]])
        st["knt_free"][r] = v_kb
        ch["v_h2"] = v_h2
        ch["v_kn"] = v_kn
        ch["v_kb"] = v_kb
        if not is_prefix:
            v_qn = op("vector", "tensor_single_scalar", s.B_QNT[r],
                      s.B_QKT[r][:, 0:16], s.B_SSN[r][:, 6:7], op=ALU.mult,
                      wait=[v_rc, st["qnt_free"][r]])
            st["qkt_free"][r] = v_qn
            st["ssn_free"][r] = v_qn
            ch["v_qn"] = v_qn

    def chain_tail(self, g):
        s = self
        r = g % 2
        op = s.op
        st = self.ast
        ch = st["ch"][r]
        m_t1 = s.mm([5], "transpose", s.pbk[5][0:16, 0:128], s.B_QNT[r],
                    s.C_IDT[:, 0:128], wait=[ch["v_qn"]], first=True,
                    last=True)
        v_t1 = s.bank_read("vector", "tensor_copy", [5], s.B_QNR,
                           s.pbk[5][0:16, 0:128], wait=[st["qnr_free"]])
        m_t2 = s.mm([5], "transpose", s.pbk[5][0:16, 0:128], s.B_KNT[r],
                    s.C_IDT[:, 0:128], wait=[ch["v_kn"]], first=True,
                    last=True)
        v_t2 = s.bank_read("vector", "tensor_copy", [5], s.B_KNR,
                           s.pbk[5][0:16, 0:128], wait=[st["knr_free"]])
        st["qnt_free"][r] = m_t1
        st["knt_free"][r] = [ch["v_kb"], m_t2]
        v_q0 = op("vector", "tensor_tensor", s.B_QT[0], s.B_QNR, s.F_Q0,
                  op=ALU.mult, wait=[v_t1, st["qt_free"], self.setupA])
        v_prev = v_q0
        for dd in range(1, 4):
            v_prev = op("vector", "tensor_single_scalar", s.B_QT[dd],
                        s.B_QT[dd - 1], s.G32_C, op=ALU.mult,
                        wait=[v_prev])
        v_kt = op("vector", "tensor_tensor", s.B_KT_, s.B_KNR, s.F_KT,
                  op=ALU.mult, wait=[v_t2, st["kt_free"]])
        v_q2 = op("vector", "tensor_tensor", s.B_Q2[r], s.B_QNR, s.F_Q2,
                  op=ALU.mult, wait=[v_t1, st["q2_free"][r]])
        st["qnr_free"] = v_q2
        st["knr_free"] = v_kt
        ch["v_prev"] = v_prev
        ch["v_kt"] = v_kt

    def body_A(self, g):
        s = self
        r = g % 2
        op = s.op
        st = self.ast
        ch = st["ch"][r]
        lastA = None
        for bj in range(4):
            for bi in range(4):
                dd = max(bi - bj, 0)
                w = [ch["v_prev"], ch["v_kt"]] if (bi == 0 and bj == 0) \
                    else []
                lastA = s.mm([0], "matmul",
                             s.pbk[0][bj * 32:(bj + 1) * 32,
                                      bi * 32:bi * 32 + 32],
                             s.B_KT_[:, bj * 32:(bj + 1) * 32],
                             s.B_QT[dd][:, bi * 32:(bi + 1) * 32],
                             start=(bi == 0), stop=(bi == 3),
                             tile_position=(0, 32 * bj),
                             wait=w, first=(bi == 0 and bj == 0),
                             last=(bi == 3 and bj == 3))
        v_am = s.bank_read("vector", "tensor_tensor", [0], s.B_ASB,
                           s.pbk[0][:, 0:128], s.C_TRIU, op=ALU.mult,
                           wait=[st["asb_free"]])
        v_ac = op("vector", "tensor_tensor", s.B_ASB, s.B_ASB, s.C_CONV,
                  op=ALU.add, wait=[v_am])
        st["qt_free"] = v_am
        st["kt_free"] = v_am
        ch["v_ac"] = v_ac

    def body_decay(self, g, half):
        s = self
        r = g % 2
        st = self.ast
        ch = st["ch"][r]
        bank = s.pbk[1 + half]
        lastd = None
        for dq in range(4):
            c = half * 4 + dq
            dsl = slice(c * 128, (c + 1) * 128)
            osl = slice(dq * 128, (dq + 1) * 128)
            w0 = [ch["v_ac"], ch["v_h2"], st["hb"][(g + 1) % 2],
                  st["s_upd"]] if dq == 0 else []
            s.mm([1 + half], "matmul", bank[:, osl],
                 s.B_HTM[r][:, dsl], s.B_ASB,
                 start=(dq == 0), stop=False, wait=w0,
                 first=(dq == 0))
            s.mm([1 + half], "matmul", bank[:, osl],
                 s.B_HB[(g + 1) % 2][0:3, dsl], s.C_BND,
                 start=False, stop=False)
            lastd = s.mm([1 + half], "matmul", bank[:, osl],
                         s.S_SB[0:16, dsl], s.B_Q2[r],
                         start=False, stop=(dq == 3),
                         last=(dq == 3))
        return lastd

    def body_evac(self, g):
        s = self
        st = self.ast
        evs = []
        for half in range(2):
            dst = s.B_MIX.rearrange("p (c t) -> p c t", c=NCH)[
                :, half * 4:(half + 1) * 4,
                (g % 4) * 128:(g % 4) * 128 + 128]
            e_ = s.bank_read("vector", "tensor_copy", [1 + half], dst,
                             s.pbk[1 + half].rearrange(
                                 "p (c t) -> p c t", c=4),
                             wait=[st["mix_free"]])
            evs.append(e_)
        return evs[1]

    def hb_dma(self, g, tm_ap):
        s = self
        r = g % 2
        return s.dma("sync", s.dHB[r], s.B_HB[r],
                     tm_ap[g * 128 + 125:g * 128 + 128, :],
                     wait=[s.ast["hb_cons"][r]])

    def body_state(self, g, is_prefix, tm_ap, t_hb=None):
        s = self
        r = g % 2
        op = s.op
        st = self.ast
        ch = st["ch"][r]
        m_s0 = s.mm([6], "matmul", s.pbk[6][0:16, :], s.B_KB[r],
                    s.B_HTM[r][:, 0:512], start=True, stop=True,
                    wait=[ch["v_kb"], ch["v_h2"]], first=True, last=True)
        m_s1 = s.mm([7], "matmul", s.pbk[7][0:16, :], s.B_KB[r],
                    s.B_HTM[r][:, 512:1024], start=True, stop=True,
                    first=True, last=True)
        st["htm_free"][r] = m_s1
        st["kb_free"][r] = m_s1
        v_s1 = op("vector", "tensor_single_scalar", s.S_SB, s.S_SB, s.G128_C,
                  op=ALU.mult, wait=[m_s1, st["s_cons"], self.setupA])
        v_s2 = s.bank_read("vector", "tensor_tensor", [6], s.S_SB[:, 0:512],
                           s.S_SB[:, 0:512], s.pbk[6][0:16, :], op=ALU.add,
                           wait=[v_s1])
        v_s3 = s.bank_read("vector", "tensor_tensor", [7], s.S_SB[:, 512:1024],
                           s.S_SB[:, 512:1024], s.pbk[7][0:16, :], op=ALU.add,
                           wait=[v_s2])
        st["s_upd"] = v_s3
        st["s_cons"] = v_s3
        if (not is_prefix) or g == NG - 1:
            if t_hb is None:
                t_hb = s.hb_dma(g, tm_ap)
            a_hs = op("scalar", "activation", out=s.HBSQ, in_=s.B_HB[r],
                      func=AF.Square, accum_out=s.B_HBST[0:3, 0:1],
                      wait=[t_hb])
            v_hm = op("vector", "tensor_scalar", out=s.B_HBST[0:3, 1:2],
                      in0=s.B_HBST[0:3, 0:1], scalar1=1.0 / D,
                      scalar2=EPS_RMS, op0=ALU.mult, op1=ALU.add, wait=[a_hs])
            a_h2 = op("scalar", "activation", out=s.B_HBST[0:3, 2:3],
                      in_=s.B_HBST[0:3, 1:2], func=AF.Sqrt, wait=[v_hm])
            v_hr = op("vector", "reciprocal", s.B_HBST[0:3, 3:4],
                      s.B_HBST[0:3, 2:3], wait=[a_h2])
            v_hb1 = op("vector", "tensor_single_scalar", s.B_HB[r], s.B_HB[r],
                       s.B_HBST[0:3, 3:4], op=ALU.mult, wait=[v_hr])
            v_hb2 = op("vector", "tensor_tensor", s.B_HB[r], s.B_HB[r],
                       s.C_W1BC[0:3, :], op=ALU.mult, wait=[v_hb1])
            st["hb"][r] = v_hb2

    def init_attn_state(self):
        empty_ch = lambda: {"v_h2": None, "v_kn": None, "v_kb": None,
                            "v_qn": None, "v_prev": None, "v_kt": None,
                            "v_ac": None}
        self.ast = {
            "hraw_free": [None, None], "htm_free": [None, None],
            "qkt_free": [None, None], "hb": [None, None],
            "hb_cons": [None, None], "s_upd": None, "s_cons": None,
            "ssn_free": [None, None], "invt_free": [None, None],
            "qnt_free": [None, None], "knt_free": [None, None],
            "kb_free": [None, None], "sq_free": [None, None],
            "q2_free": [None, None],
            "qnr_free": None, "knr_free": None, "qt_free": None,
            "kt_free": None, "asb_free": None, "mix_free": None,
            "ch": [empty_ch(), empty_ch()],
        }

    def p_prefix(self):
        s = self
        last_mm, t_qke = s.qkraw_tile(0, s.hp_fm)
        s.chain_stats(0, s.hp_tm, True, t_qke)
        for g in range(NG):
            nx = g + 1
            if nx < NG:
                if nx % 4 == 0:
                    self.qke_free = (s.sT, s.sT.n)
                    last_mm, t_qke = s.qkraw_tile(nx // 4, s.hp_fm)
                s.chain_stats(nx, s.hp_tm, True, t_qke)
            s.body_state(g, True, s.hp_tm)
        self.qke_free = (s.sT, s.sT.n)

    def proj_tile(self, tau, mix_ev):
        s = self
        op = s.op
        fmv = s.h_fm.rearrange("(c p) t -> p c t", p=128)
        sp = []
        pend = None
        for c in range(NCH):
            src = s.B_MIX[:, c * 512:(c + 1) * 512]
            a_h = op("scalar", "copy", out=s.HN_HI[:, c * 512:(c + 1) * 512],
                     in_=src, wait=[mix_ev, self.hn_free])
            if pend is not None:
                psrc, pc, pa = pend
                sp.append(op("vector", "tensor_sub",
                             s.HN_LO[:, pc * 512:(pc + 1) * 512], psrc,
                             s.HN_HI[:, pc * 512:(pc + 1) * 512], wait=[pa]))
            pend = (src, c, a_h)
        psrc, pc, pa = pend
        sp.append(op("vector", "tensor_sub",
                     s.HN_LO[:, pc * 512:(pc + 1) * 512], psrc,
                     s.HN_HI[:, pc * 512:(pc + 1) * 512], wait=[pa]))
        self.ast["mix_free"] = [sp[-1], (s.sA, s.sA.n)]
        wslot = self.wd_slot
        for kk in range(NCH):
            r = kk % 2
            th = s.dma("sync", s.dWD[r], s.WD_HI[r], s.sc_proj[0][kk],
                       wait=[wslot[r], self.split_done[("proj", kk)]])
            tl = s.dma("sync", s.dWD[r], s.WD_LO[r], s.sc_proj[1][kk])
            used = None
            for dout in range(NCH):
                lh = s.WD_HI[r][:, dout * 128:(dout + 1) * 128]
                ll = s.WD_LO[r][:, dout * 128:(dout + 1) * 128]
                xh = s.HN_HI[:, kk * 512:(kk + 1) * 512]
                xl = s.HN_LO[:, kk * 512:(kk + 1) * 512]
                w = [tl, sp[kk]] if dout == 0 else []
                s.mm([dout], "matmul", s.pbk[dout], lh, xh,
                     start=(kk == 0), stop=False, wait=w, first=(kk == 0))
                s.mm([dout], "matmul", s.pbk[dout], lh, xl,
                     start=False, stop=False)
                used = s.mm([dout], "matmul", s.pbk[dout], ll, xh,
                            start=False, stop=(kk == NCH - 1),
                            last=(kk == NCH - 1))
            wslot[r] = used
        self.hn_free = used
        h2t = []
        for dout in range(NCH):
            r2 = dout % 2
            t_h = s.dma("sync", s.dHF[r2], s.HFT[:, r2 * 512:(r2 + 1) * 512],
                        fmv[:, dout, tau * TT:(tau + 1) * TT],
                        wait=[self.hftr_free[r2]])
            a_e = s.bank_read("scalar", "activation", [dout],
                              out=s.G_TMP[dout % 2], in_=s.pbk[dout],
                              func=AF.Identity, bias=s.PB_[:, dout:dout + 1],
                              wait=[self.gtmp_free[dout % 2], self.setupW])
            v_a = op("vector", "tensor_add",
                     s.AR_A[:, dout * 512:(dout + 1) * 512],
                     s.G_TMP[dout % 2],
                     s.HFT[:, r2 * 512:(r2 + 1) * 512],
                     wait=[a_e, t_h, self.ara_free])
            self.gtmp_free[dout % 2] = v_a
            self.hftr_free[r2] = v_a
            h2t.append(v_a)
        return h2t

    def rms_stats(self, srcs, eps):
        s = self
        op = s.op
        lastm = None
        for c, (src, w) in enumerate(srcs):
            ww = list(w) + [self.gtmp_free[c % 2]]
            if c % 2 == 0:
                a_sq = op("scalar", "activation", out=s.G_TMP[c % 2],
                          in_=src, func=AF.Square, wait=ww)
            else:
                a_sq = op("vector", "tensor_tensor", s.G_TMP[c % 2],
                          src, src, op=ALU.mult, wait=ww)
            lastm = s.mm([5], "matmul", s.pbk[5][0:1, :], s.ONES_C,
                         s.G_TMP[c % 2], start=(c == 0), stop=(c == 7),
                         wait=[a_sq], first=(c == 0), last=(c == 7))
            self.gtmp_free[c % 2] = lastm
        v_m = s.bank_read("vector", "tensor_scalar", [5], out=s.MROW,
                          in0=s.pbk[5][0:1, :], scalar1=1.0 / D, scalar2=eps,
                          op0=ALU.mult, op1=ALU.add, wait=[self.row_free])
        a_s = op("scalar", "activation", out=s.SQROW, in_=s.MROW,
                 func=AF.Sqrt, wait=[v_m])
        v_i = op("vector", "reciprocal", s.IROW, s.SQROW, wait=[a_s])
        m_b = s.mm([2], "matmul", s.pbk[2], s.ONES_R, s.IROW,
                   start=True, stop=True, wait=[v_i], first=True, last=True)
        self.row_free = m_b
        return m_b

    def ln_tile(self, tau, ticks_in):
        s = self
        op = s.op
        for c in range(NCH):
            src = s.AR_A[:, c * 512:(c + 1) * 512]
            w = [ticks_in[c]] if ticks_in else []
            s.mm([6], "matmul", s.pbk[6][0:1, :], s.ONES_C, src,
                 start=(c == 0), stop=(c == 7), wait=w,
                 first=(c == 0), last=(c == 7))
            ww = list(w) + [self.gtmp_free[c % 2]]
            if c % 2 == 0:
                a_sq = op("scalar", "activation", out=s.G_TMP[c % 2],
                          in_=src, func=AF.Square, wait=ww)
            else:
                a_sq = op("vector", "tensor_tensor", s.G_TMP[c % 2],
                          src, src, op=ALU.mult, wait=ww)
            lastq = s.mm([3], "matmul", s.pbk[3][0:1, :], s.ONES_C,
                         s.G_TMP[c % 2], start=(c == 0), stop=(c == 7),
                         wait=[a_sq], first=(c == 0), last=(c == 7))
            self.gtmp_free[c % 2] = lastq
        v_m = s.bank_read("vector", "tensor_scalar", [6], out=s.MROW,
                          in0=s.pbk[6][0:1, :], scalar1=1.0 / D, scalar2=None,
                          op0=ALU.mult, wait=[self.row_free])
        v_v = s.bank_read("vector", "tensor_scalar", [3], out=s.VROW,
                          in0=s.pbk[3][0:1, :], scalar1=1.0 / D, scalar2=None,
                          op0=ALU.mult)
        a_m2 = op("scalar", "activation", out=s.SQROW, in_=s.MROW,
                  func=AF.Square, wait=[v_m])
        v_v2 = op("vector", "tensor_sub", s.VROW, s.VROW, s.SQROW,
                  wait=[v_v, a_m2])
        v_v3 = op("vector", "tensor_scalar", out=s.VROW, in0=s.VROW,
                  scalar1=1.0, scalar2=EPS_LN, op0=ALU.mult, op1=ALU.add,
                  wait=[v_v2])
        a_sd = op("scalar", "activation", out=s.SQROW, in_=s.VROW,
                  func=AF.Sqrt, wait=[v_v3])
        v_i = op("vector", "reciprocal", s.IROW, s.SQROW, wait=[a_sd])
        v_mi = op("vector", "tensor_mul", s.MIROW, s.MROW, s.IROW, wait=[v_i])
        m_b1 = s.mm([2], "matmul", s.pbk[2], s.ONES_R, s.IROW,
                    start=True, stop=True, wait=[v_i], first=True, last=True)
        m_b2 = s.mm([4], "matmul", s.pbk[4], s.ONES_R, s.MIROW,
                    start=True, stop=True, wait=[v_mi, self.qke_free],
                    first=True, last=True)
        self.row_free = m_b2
        outt = []
        for c in range(NCH):
            src = s.AR_A[:, c * 512:(c + 1) * 512]
            v_1 = s.bank_read("vector", "tensor_tensor", [2],
                              s.G_TMP[c % 2], src, s.pbk[2], op=ALU.mult,
                              wait=[self.gtmp_free[c % 2]])
            v_2 = s.bank_read("vector", "tensor_sub", [4], src,
                              s.G_TMP[c % 2], s.pbk[4], wait=[v_1])
            self.gtmp_free[c % 2] = v_2
            outt.append(v_2)
        return outt

    def mlp_tile(self, tau, h2t):
        s = self
        op = s.op
        s.rms_stats([(s.AR_A[:, c * 512:(c + 1) * 512], [h2t[c]])
                     for c in range(NCH)], EPS_RMS)
        fmv = s.h_fm.rearrange("(c p) t -> p c t", p=128)
        zt = [None] * NCH
        for sub in range(2):
            ssl = slice(sub * 256, sub * 256 + 256)
            hnt = []
            pend = None
            for c in range(NCH):
                csl = slice(c * 512 + sub * 256, c * 512 + sub * 256 + 256)
                hsl = slice(c * 256, (c + 1) * 256)
                v_x = s.bank_read("vector", "scalar_tensor_tensor", [2],
                                  s.G_TMP[c % 2][:, 0:256],
                                  s.AR_A[:, csl],
                                  s.N2W[:, c:c + 1], s.pbk[2][:, ssl],
                                  op0=ALU.mult, op1=ALU.mult,
                                  wait=[self.gtmp_free[c % 2], self.hn_free])
                a_h = op("scalar", "copy", out=s.HS_HI[:, hsl],
                         in_=s.G_TMP[c % 2][:, 0:256], wait=[v_x])
                if pend is not None:
                    pc, phsl, pa = pend
                    v_l = op("vector", "tensor_sub", s.HS_LO[:, phsl],
                             s.G_TMP[pc % 2][:, 0:256], s.HS_HI[:, phsl],
                             wait=[pa])
                    self.gtmp_free[pc % 2] = v_l
                    hnt.append(v_l)
                pend = (c, hsl, a_h)
            pc, phsl, pa = pend
            v_l = op("vector", "tensor_sub", s.HS_LO[:, phsl],
                     s.G_TMP[pc % 2][:, 0:256], s.HS_HI[:, phsl],
                     wait=[pa])
            self.gtmp_free[pc % 2] = v_l
            hnt.append(v_l)
            wslot2 = self.ws_slot
            slot4 = [wslot2[0], wslot2[1], wslot2[0], wslot2[1]]
            sems4 = [s.dWS[0], s.dWS[1], s.dW2[0], s.dW2[1]]
            w2pre = [None] * 4
            for k2 in range(2):
                rq = k2 % 2
                w2pre[k2] = (
                    s.dma("sync", sems4[k2], s.WS_HI[rq][:, 0:1024],
                          s.sc_w2[0][k2],
                          wait=[slot4[k2], self.split_done[("w2", k2)]]),
                    s.dma("sync", sems4[k2], s.WS_LO[rq][:, 0:1024],
                          s.sc_w2[1][k2]))
            wslot = self.wd_slot
            midt = []
            mpend = None
            for cp in range(MHC):
                r = cp % 2
                th = s.dma("sync", s.dWD[r], s.WD_HI[r], s.sc_w1[0][cp],
                           wait=[wslot[r], self.split_done[("w1", cp)]])
                tl = s.dma("sync", s.dWD[r], s.WD_LO[r], s.sc_w1[1][cp])
                bank = cp % 2
                lastw = None
                for kk in range(NCH):
                    lh = s.WD_HI[r][:, kk * 128:(kk + 1) * 128]
                    ll = s.WD_LO[r][:, kk * 128:(kk + 1) * 128]
                    xh = s.HS_HI[:, kk * 256:(kk + 1) * 256]
                    xl = s.HS_LO[:, kk * 256:(kk + 1) * 256]
                    w = ([tl, self.mid_free] if kk == 0 else []) + [hnt[kk]]
                    s.mm([bank], "matmul", s.pbk[bank][:, 0:256], lh, xh,
                         start=(kk == 0), stop=False, wait=w, first=(kk == 0))
                    s.mm([bank], "matmul", s.pbk[bank][:, 0:256], lh, xl,
                         start=False, stop=False)
                    lastw = s.mm([bank], "matmul", s.pbk[bank][:, 0:256],
                                 ll, xh, start=False, stop=(kk == NCH - 1),
                                 last=(kk == NCH - 1))
                wslot[r] = lastw
                msl = slice(cp * 256, (cp + 1) * 256)
                a_g = s.bank_read("scalar", "activation", [bank],
                                  out=s.G_TMP[cp % 2][:, 0:256],
                                  in_=s.pbk[bank][:, 0:256],
                                  func=AF.Gelu_apprx_tanh,
                                  bias=s.B1_[:, cp:cp + 1],
                                  wait=[self.gtmp_free[cp % 2]])
                a_h = op("scalar", "copy", out=s.M_HI[:, msl],
                         in_=s.G_TMP[cp % 2][:, 0:256], wait=[a_g])
                if mpend is not None:
                    pcp, pmsl, pa = mpend
                    v_l = op("vector", "tensor_sub", s.M_LO[:, pmsl],
                             s.G_TMP[pcp % 2][:, 0:256], s.M_HI[:, pmsl],
                             wait=[pa])
                    self.gtmp_free[pcp % 2] = v_l
                    midt.append(v_l)
                mpend = (cp, msl, a_h)
                if cp == 1:
                    for k2x in (2, 3):
                        rqx = k2x % 2
                        offx = (k2x // 2) * 1024
                        w2pre[k2x] = (
                            s.dma("sync", sems4[k2x],
                                  s.WS_HI[rqx][:, offx:offx + 1024],
                                  s.sc_w2[0][k2x],
                                  wait=[slot4[k2x],
                                        self.split_done[("w2", k2x)]]),
                            s.dma("sync", sems4[k2x],
                                  s.WS_LO[rqx][:, offx:offx + 1024],
                                  s.sc_w2[1][k2x]))
            pcp, pmsl, pa = mpend
            v_l = op("vector", "tensor_sub", s.M_LO[:, pmsl],
                     s.G_TMP[pcp % 2][:, 0:256], s.M_HI[:, pmsl],
                     wait=[pa])
            self.gtmp_free[pcp % 2] = v_l
            midt.append(v_l)
            self.hn_free = wslot[(MHC - 1) % 2]
            used = None
            for k2 in range(MHC):
                idx = k2 % 4
                r = idx % 2
                off = (idx // 2) * 1024
                if k2 < 4:
                    th, tl = w2pre[idx]
                else:
                    th = s.dma("sync", sems4[idx],
                               s.WS_HI[r][:, off:off + 1024],
                               s.sc_w2[0][k2],
                               wait=[slot4[idx],
                                     self.split_done[("w2", k2)]])
                    tl = s.dma("sync", sems4[idx],
                               s.WS_LO[r][:, off:off + 1024],
                               s.sc_w2[1][k2])
                for dout in range(NCH):
                    lh = s.WS_HI[r][:, off + dout * 128:
                                    off + (dout + 1) * 128]
                    ll = s.WS_LO[r][:, off + dout * 128:
                                    off + (dout + 1) * 128]
                    xh = s.M_HI[:, k2 * 256:(k2 + 1) * 256]
                    xl = s.M_LO[:, k2 * 256:(k2 + 1) * 256]
                    w = [tl, midt[k2]] if dout == 0 else []
                    s.mm([dout], "matmul", s.pbk[dout][:, ssl], lh, xh,
                         start=(k2 == 0), stop=False, wait=w,
                         first=(k2 == 0))
                    s.mm([dout], "matmul", s.pbk[dout][:, ssl], lh, xl,
                         start=False, stop=False)
                    used = s.mm([dout], "matmul", s.pbk[dout][:, ssl],
                                ll, xh, start=False, stop=(k2 == MHC - 1),
                                last=(k2 == MHC - 1))
                slot4[idx] = used
            wslot2[0] = [slot4[0], slot4[2]]
            wslot2[1] = [slot4[1], slot4[3]]
            self.mid_free = used
            pend = None

            def z_add(dout, t_h, a_e):
                csl = slice(dout * 512 + sub * 256,
                            dout * 512 + sub * 256 + 256)
                r2 = dout % 2
                v_1 = op("vector", "tensor_add", s.G_TMP[dout % 2][:, 0:256],
                         s.G_TMP[dout % 2][:, 0:256],
                         s.HFT[:, r2 * 512 + sub * 256:
                               r2 * 512 + sub * 256 + 256],
                         wait=[a_e, t_h])
                v_2 = op("vector", "tensor_add", s.AR_A[:, csl],
                         s.AR_A[:, csl], s.G_TMP[dout % 2][:, 0:256],
                         wait=[v_1])
                self.gtmp_free[dout % 2] = v_2
                self.hftr_free[r2] = v_1
                zt[dout] = v_2

            for dout in range(NCH):
                r2 = dout % 2
                t_h = s.dma("sync", s.dHF[r2],
                            s.HFT[:, r2 * 512 + sub * 256:
                                  r2 * 512 + sub * 256 + 256],
                            fmv[:, dout, tau * TT + sub * 256:
                                tau * TT + sub * 256 + 256],
                            wait=[self.hftr_free[r2]])
                a_e = s.bank_read("scalar", "activation", [dout],
                                  out=s.G_TMP[dout % 2][:, 0:256],
                                  in_=s.pbk[dout][:, ssl],
                                  func=AF.Identity,
                                  bias=s.B2_[:, dout:dout + 1],
                                  wait=[self.gtmp_free[dout % 2]])
                if pend is not None:
                    z_add(*pend)
                pend = (dout, t_h, a_e)
            z_add(*pend)
        return zt

    def ffn_tile(self, tau, yt):
        s = self
        op = s.op
        m_b = s.rms_stats([(s.AR_A[:, c * 512:(c + 1) * 512], [yt[c]])
                           for c in range(NCH)], EPS_RMS)
        lastsc = None
        xnt = []
        pend = None
        for c in range(NCH):
            rr = c % 2
            v_x = s.bank_read("vector", "scalar_tensor_tensor", [2],
                              s.G_XNF[rr],
                              s.AR_A[:, c * 512:(c + 1) * 512],
                              s.FSCW[:, c:c + 1], s.pbk[2],
                              op0=ALU.mult, op1=ALU.mult,
                              wait=[self.xnf_free[rr], self.hn_free])
            a_h = op("scalar", "copy",
                     out=s.XN_HI[:, c * 512:(c + 1) * 512],
                     in_=s.G_XNF[rr], wait=[v_x])
            lastsc = s.mm([5], "matmul", s.pbk[5][0:8, :],
                          s.C_CT[:, c * 8:(c + 1) * 8], s.G_XNF[rr],
                          start=(c == 0), stop=(c == 7),
                          wait=[v_x] + ([self.setupT] if c == 0 else []),
                          first=(c == 0), last=(c == 7))
            if pend is not None:
                pc, pa, psc = pend
                v_l = op("vector", "tensor_sub",
                         s.XN_LO[:, pc * 512:(pc + 1) * 512],
                         s.G_XNF[pc % 2],
                         s.XN_HI[:, pc * 512:(pc + 1) * 512], wait=[pa])
                self.xnf_free[pc % 2] = [psc, v_l]
                xnt.append(v_l)
            pend = (c, a_h, lastsc)
        pc, pa, psc = pend
        v_l = op("vector", "tensor_sub",
                 s.XN_LO[:, pc * 512:(pc + 1) * 512], s.G_XNF[pc % 2],
                 s.XN_HI[:, pc * 512:(pc + 1) * 512], wait=[pa])
        self.xnf_free[pc % 2] = [psc, v_l]
        xnt.append(v_l)
        a_sc = s.bank_read("scalar", "activation", [5], out=s.SC_SB,
                           in_=s.pbk[5][0:8, :], func=AF.Identity,
                           bias=s.CSQ, wait=[self.scsb_free])
        mskt = []
        for sub in range(4):
            m_t = s.mm([3], "transpose", s.pbk[3][:, 0:8],
                       s.SC_SB[0:8, sub * 128:(sub + 1) * 128],
                       s.C_IDT[0:8, 0:8], wait=[a_sc], first=True, last=True)
            v_c = s.bank_read("vector", "tensor_copy", [3], s.SCTM,
                              s.pbk[3][:, 0:8], wait=[self.sctm_free])
            v_mn = op("vector", "tensor_reduce", s.MNV, s.SCTM, axis=AX.X,
                      op=ALU.min, wait=[v_c])
            v_eq = op("vector", "tensor_single_scalar", s.EQT, s.SCTM, s.MNV,
                      op=ALU.is_equal, wait=[v_mn])
            v_pr = op("vector", "memset", s.NYT, 1.0, wait=[v_eq])
            for e in range(NE):
                v_ex = op("vector", "tensor_mul", s.EXC[:, e:e + 1],
                          s.EQT[:, e:e + 1], s.NYT, wait=[v_pr])
                v_pr = op("vector", "tensor_sub", s.NYT, s.NYT,
                          s.EXC[:, e:e + 1], wait=[v_ex])
            self.sctm_free = v_pr
            m_tb = s.mm([3], "transpose", s.pbk[3][0:8, 0:128], s.EXC,
                        s.C_IDT[:, 0:128], wait=[v_pr], first=True,
                        last=True)
            v_mk = s.bank_read("vector", "tensor_copy", [3],
                               s.MSK_B[0:8, sub * 128:(sub + 1) * 128],
                               s.pbk[3][0:8, 0:128], wait=[self.msk_free])
            mskt.append(v_mk)
        self.scsb_free = m_tb
        z2t = list(yt)
        first_e = [True] * NCH
        last_mb = None
        uppre = [None, None]
        for e in range(NE):
            for cpq in range(2):
                rq = cpq % 2
                uppre[rq] = (
                    s.dma("sync", s.dWS[rq], s.WS_HI[rq], s.sc_up[0][e, cpq],
                          wait=[self.ws_slot[rq],
                                self.split_done[("up", e, cpq, 0)],
                                self.split_done[("up", e, cpq, 1)]]),
                    s.dma("sync", s.dWS[rq], s.WS_LO[rq], s.sc_up[1][e, cpq]))
            m_mb = s.mm([5], "matmul", s.pbk[5],
                        s.BE_B[0:8, e * 128:(e + 1) * 128],
                        s.MSK_B[0:8, 0:512], start=True, stop=True,
                        wait=[mskt[-1], self.setupW], first=True, last=True)
            last_mb = m_mb
            v_mb = s.bank_read("vector", "tensor_copy", [5], s.MKB_B,
                               s.pbk[5], wait=[self.mkb_free])
            wslotd = self.wd_slot
            dwpre = [None, None]
            for k2q in range(2):
                rq = k2q % 2
                dwpre[rq] = (
                    s.dma("sync", s.dWD[rq], s.WD_HI[rq], s.sc_dw[0][e, k2q],
                          wait=[wslotd[rq], self.split_done[("dw", e, k2q)]]),
                    s.dma("sync", s.dWD[rq], s.WD_LO[rq], s.sc_dw[1][e, k2q]))
            wslot = self.ws_slot
            fmidt = []
            for cp in range(FHC):
                r = cp % 2
                if cp < 2:
                    th, tl = uppre[r]
                else:
                    th = s.dma("sync", s.dWS[r], s.WS_HI[r], s.sc_up[0][e, cp],
                               wait=[wslot[r],
                                     self.split_done[("up", e, cp, 0)],
                                     self.split_done[("up", e, cp, 1)]])
                    tl = s.dma("sync", s.dWS[r], s.WS_LO[r], s.sc_up[1][e, cp])
                pa, pg = (cp % 2) * 2, (cp % 2) * 2 + 1
                lastu = None
                for half, bank in ((0, pa), (1, pg)):
                    for kk in range(NCH):
                        lh = s.WS_HI[r][:, kk * 256 + half * 128:
                                        kk * 256 + half * 128 + 128]
                        ll = s.WS_LO[r][:, kk * 256 + half * 128:
                                        kk * 256 + half * 128 + 128]
                        xh = s.XN_HI[:, kk * 512:(kk + 1) * 512]
                        xl = s.XN_LO[:, kk * 512:(kk + 1) * 512]
                        w = [tl, xnt[-1]] if kk == 0 else []
                        s.mm([bank], "matmul", s.pbk[bank], lh, xh,
                             start=(kk == 0), stop=False, wait=w,
                             first=(kk == 0))
                        s.mm([bank], "matmul", s.pbk[bank], lh, xl,
                             start=False, stop=False)
                        lastu = s.mm([bank], "matmul", s.pbk[bank], ll, xh,
                                     start=False, stop=(kk == NCH - 1),
                                     last=(kk == NCH - 1))
                wslot[r] = lastu
                a_si = s.bank_read("scalar", "activation", [pg],
                                   out=s.G_TMP[cp % 2], in_=s.pbk[pg],
                                   func=AF.Silu,
                                   wait=[self.gtmp_free[cp % 2]])
                v_mu = s.bank_read("vector", "tensor_tensor", [pa],
                                   s.G_XNF[cp % 2], s.pbk[pa],
                                   s.G_TMP[cp % 2], op=ALU.mult,
                                   wait=[a_si, self.xnf_free[cp % 2],
                                         self.fmid_free])
                v_mu = op("vector", "tensor_mul", s.G_XNF[cp % 2],
                          s.G_XNF[cp % 2], s.MKB_B, wait=[v_mu, v_mb])
                self.mkb_free = v_mu
                a_mh = op("scalar", "copy",
                          out=s.FMID_HI[:, cp * 512:(cp + 1) * 512],
                          in_=s.G_XNF[cp % 2], wait=[v_mu])
                v_ml = op("vector", "tensor_sub",
                          s.FMID_LO[:, cp * 512:(cp + 1) * 512],
                          s.G_XNF[cp % 2],
                          s.FMID_HI[:, cp * 512:(cp + 1) * 512],
                          wait=[a_mh])
                self.gtmp_free[cp % 2] = v_mu
                self.xnf_free[cp % 2] = [v_ml, (s.sA, s.sA.n)]
                fmidt.append(v_ml)
            self.hn_free = wslot[(FHC - 1) % 2]
            wslot = self.wd_slot
            used = None
            for k2 in range(FHC):
                r = k2 % 2
                if k2 < 2:
                    th, tl = dwpre[r]
                else:
                    th = s.dma("sync", s.dWD[r], s.WD_HI[r], s.sc_dw[0][e, k2],
                               wait=[wslot[r], self.split_done[("dw", e, k2)]])
                    tl = s.dma("sync", s.dWD[r], s.WD_LO[r], s.sc_dw[1][e, k2])
                for dout in range(NCH):
                    lh = s.WD_HI[r][:, dout * 128:(dout + 1) * 128]
                    ll = s.WD_LO[r][:, dout * 128:(dout + 1) * 128]
                    xh = s.FMID_HI[:, k2 * 512:(k2 + 1) * 512]
                    xl = s.FMID_LO[:, k2 * 512:(k2 + 1) * 512]
                    w = [tl, fmidt[k2]] if dout == 0 else []
                    s.mm([dout], "matmul", s.pbk[dout], lh, xh,
                         start=(k2 == 0), stop=False, wait=w,
                         first=(k2 == 0))
                    s.mm([dout], "matmul", s.pbk[dout], lh, xl,
                         start=False, stop=False)
                    used = s.mm([dout], "matmul", s.pbk[dout], ll, xh,
                                start=False, stop=(k2 == FHC - 1),
                                last=(k2 == FHC - 1))
                wslot[r] = used
            self.fmid_free = used
            nz = []
            for dout in range(NCH):
                if first_e[dout]:
                    v_z = s.bank_read("vector", "scalar_tensor_tensor",
                                      [dout],
                                      s.AR_A[:, dout * 512:(dout + 1) * 512],
                                      s.AR_A[:, dout * 512:(dout + 1) * 512],
                                      2.0, s.pbk[dout],
                                      op0=ALU.mult, op1=ALU.add,
                                      wait=[z2t[dout]])
                    first_e[dout] = False
                else:
                    v_z = s.bank_read("vector", "tensor_add", [dout],
                                      s.AR_A[:, dout * 512:(dout + 1) * 512],
                                      s.AR_A[:, dout * 512:(dout + 1) * 512],
                                      s.pbk[dout], wait=[z2t[dout]])
                nz.append(v_z)
            z2t = nz
        self.msk_free = last_mb
        return z2t

    def p_main(self):
        s = self
        st = self.ast
        last_mm, t_qke = s.qkraw_tile(0, s.h_fm)
        s.chain_stats(0, s.h_tm, False, t_qke)
        s.chain_tail(0)
        for tau in range(NTT):
            mix_ev = None
            for g in range(tau * 4, tau * 4 + 4):
                s.body_A(g)
                t_hb = s.hb_dma(g, s.h_tm)
                nx = g + 1
                if nx < NG:
                    if nx % 4 == 0:
                        self.qke_free = (s.sT, s.sT.n)
                        last_mm, t_qke = s.qkraw_tile(nx // 4, s.h_fm)
                    s.chain_stats(nx, s.h_tm, False, t_qke)
                s.body_decay(g, 0)
                if nx < NG:
                    s.chain_tail(nx)
                ld1 = s.body_decay(g, 1)
                st["q2_free"][g % 2] = ld1
                st["asb_free"] = ld1
                st["hb_cons"][(g + 1) % 2] = ld1
                mix_ev = s.body_evac(g)
                s.body_state(g, False, s.h_tm, t_hb)
            self.qke_free = (s.sT, s.sT.n)
            h2t = s.proj_tile(tau, mix_ev)
            s.tap("h2_%d" % tau, s.AR_A, (128, NCH * TT), wait=h2t)
            zt = s.mlp_tile(tau, h2t)
            s.tap("z_%d" % tau, s.AR_A, (128, NCH * TT), wait=zt)
            yt = s.ln_tile(tau, zt)
            s.tap("y_%d" % tau, s.AR_A, (128, NCH * TT), wait=yt)
            z2t = s.ffn_tile(tau, yt)
            ot = s.ln_tile(tau, z2t)
            s.tap("o_%d" % tau, s.AR_A, (128, NCH * TT), wait=ot)
            sts = []
            for c in range(NCH):
                sts.append(s.dma("scalar", s.dO,
                                 s.y_out[c * 128:(c + 1) * 128,
                                         tau * TT:(tau + 1) * TT],
                                 s.AR_A[:, c * 512:(c + 1) * 512],
                                 wait=[ot[c]]))
            self.ara_free = sts[-1]
        self.waitv("scalar", (s.dO, s.dO.n))
        self.waitv("gpsimd", (s.dGs[0], s.dGs[0].n), (s.dGs[1], s.dGs[1].n))

    def build(self):
        self.p_setup()
        self.p_split()
        self.init_attn_state()
        self.p_prefix()
        self.p_main()
        self.emit()
        return self.nc

    def emit(self):
        with self.nc.Block() as block:
            for name in self.ENGS:
                getattr(block, name)(
                    (lambda fns: lambda e: [f(e) for f in fns])(self.em[name]))


def _host_inputs(inputs):
    x = np.ascontiguousarray(np.asarray(inputs["L0_attn_h"], dtype=np.float32))
    u = np.asarray(inputs["L0_attn_u"], dtype=np.float32)
    v = np.asarray(inputs["L0_attn_v"], dtype=np.float32)
    uvc = np.ascontiguousarray(np.concatenate([u, v], axis=1))
    projT = np.ascontiguousarray(np.asarray(inputs["L0_attn_proj_w"],
                                            dtype=np.float32).T)
    w1 = np.asarray(inputs["L0_attn_mlp_w1"], dtype=np.float32)
    w1h = np.ascontiguousarray(
        w1.reshape(8, 128, MHC, 128).transpose(2, 1, 0, 3)
        .reshape(MHC, 128, 8 * 128))
    w2w = np.ascontiguousarray(np.asarray(inputs["L0_attn_mlp_w2"],
                                          dtype=np.float32))
    up = np.asarray(inputs["L0_ffn_up_w"], dtype=np.float32)
    upb = np.ascontiguousarray(
        up.reshape(NE, 8, 128, 2, FHC, 128)
        .transpose(0, 4, 2, 1, 3, 5)
        .reshape(NE, FHC, 128, 8 * 256))
    dwn = np.ascontiguousarray(np.asarray(inputs["L0_ffn_down_w"],
                                          dtype=np.float32))
    cenT = np.ascontiguousarray(np.asarray(inputs["L0_ffn_centroids"],
                                           dtype=np.float32).T)
    w1bc = np.ascontiguousarray(np.broadcast_to(
        np.asarray(inputs["L0_attn_norm1_w"], dtype=np.float32)[None, :],
        (128, D)))
    shared = {
        "uv": uvc, "projT": projT, "w1h": w1h, "w2w": w2w, "upb": upb,
        "dwn": dwn, "cenT": cenT,
        "n1w": np.ascontiguousarray(np.asarray(inputs["L0_attn_norm1_w"], np.float32)),
        "n2w": np.ascontiguousarray(np.asarray(inputs["L0_attn_norm2_w"], np.float32)),
        "fsc": np.ascontiguousarray(np.asarray(inputs["L0_ffn_scale"], np.float32)),
        "pbv": np.ascontiguousarray(np.asarray(inputs["L0_attn_proj_b"], np.float32)),
        "b1v": np.ascontiguousarray(np.asarray(inputs["L0_attn_mlp_b1"], np.float32)),
        "b2v": np.ascontiguousarray(np.asarray(inputs["L0_attn_mlp_b2"], np.float32)),
        "dlog": np.ascontiguousarray(
            np.asarray(inputs["L0_attn_decay_logit"], np.float32)[None, :]),
        "alog": np.ascontiguousarray(
            np.asarray(inputs["L0_attn_alpha_logit"], np.float32)[None, :]),
        "glog": np.ascontiguousarray(
            np.asarray(inputs["L0_attn_k_base_gate_logit"],
                       np.float32).reshape(1, 1)),
        "kker": np.ascontiguousarray(
            np.asarray(inputs["L0_attn_k_base_kernel"], np.float32)[None, :]),
        "w1bc": w1bc,
    }
    zeros_tm = np.zeros((T, D), np.float32)
    zeros_fm = np.zeros((D, T), np.float32)
    in_maps = []
    for core in range(8):
        b, half = core // 2, core % 2
        tok = x[b, half * T:(half + 1) * T]
        m = dict(shared)
        m["h_tm"] = np.ascontiguousarray(tok)
        m["h_fm"] = np.ascontiguousarray(tok.T)
        if half == 1:
            pre = x[b, 0:T]
            m["hp_tm"] = np.ascontiguousarray(pre)
            m["hp_fm"] = np.ascontiguousarray(pre.T)
        else:
            m["hp_tm"] = zeros_tm
            m["hp_fm"] = zeros_fm
        in_maps.append(m)
    return in_maps


def _get_nc():
    key = tuple(DEBUG_TAPS)
    if key not in _NC_CACHE:
        b = B(taps=DEBUG_TAPS)
        _NC_CACHE[key] = (b.build(), b)
    return _NC_CACHE[key]


def kernel(**inputs):
    nc, _ = _get_nc()
    in_maps = _host_inputs(inputs)
    res = run_bass_kernel_spmd(nc, in_maps, core_ids=list(range(8)))
    out = np.empty((4, 2 * T, D), np.float32)
    for core in range(8):
        b, half = core // 2, core % 2
        out[b, half * T:(half + 1) * T] = res.results[core]["y_out"].T
    return out


# version 14
# speedup vs baseline: 1.0106x; 1.0106x over previous
import numpy as np
import ml_dtypes
import concourse.bass as bass
import concourse.mybir as mybir
from concourse.bass_utils import run_bass_kernel_spmd

F32 = mybir.dt.float32
BF16 = mybir.dt.bfloat16
AF = mybir.ActivationFunctionType
ALU = mybir.AluOpType
AX = mybir.AxisListType

D = 1024
NCH = 8
T = 2048
TT = 512
NTT = 4
NG = 16
R = 16
NE = 8
FH = 2048
FHC = 16
MH = 4096
MHC = 32
EPS_RMS = 1e-6
EPS_LN = 1e-5
EPS_L2 = 1e-8

DEBUG_TAPS = ()
_NC_CACHE = {}


class Sem:
    def __init__(self, nc, name):
        self.h = nc.alloc_semaphore(name)
        self.n = 0


def _flat(ws):
    out = []
    for w in ws:
        if w is None:
            continue
        if isinstance(w, list):
            out.extend(_flat(w))
        else:
            out.append(w)
    return out


class B:
    ENGS = ("tensor", "vector", "scalar", "sync", "gpsimd")

    def __init__(self, taps=()):
        nc = bass.Bass("TRN2", target_bir_lowering=False)
        self.nc = nc
        self.em = {n: [] for n in self.ENGS}
        self.sat = {n: {} for n in self.ENGS}
        self.sT = Sem(nc, "sT")
        self.sV = Sem(nc, "sV")
        self.sA = Sem(nc, "sA")
        self.sG = Sem(nc, "sG")
        self.dC = Sem(nc, "dC")
        self.dHR = [Sem(nc, "dHR0"), Sem(nc, "dHR1")]
        self.dHB = [Sem(nc, "dHB0"), Sem(nc, "dHB1")]
        self.dHF = [Sem(nc, "dHF0"), Sem(nc, "dHF1")]
        self.dWS = [Sem(nc, "dWS0"), Sem(nc, "dWS1")]
        self.dW2 = [Sem(nc, "dW20"), Sem(nc, "dW21")]
        self.dWD = [Sem(nc, "dWD0"), Sem(nc, "dWD1")]
        self.dAs = [Sem(nc, "dA0"), Sem(nc, "dA1")]
        self.dGs = [Sem(nc, "dG0"), Sem(nc, "dG1")]
        self.dO = Sem(nc, "dO")
        self.taps = {}
        self.tap_names = list(taps)
        self.bank_r = [None] * 8
        self.bank_w = [None] * 8
        self.gtmp_free = [None, None]
        self.xnf_free = [None, None]
        self.hn_free = None
        self.mid_free = None
        self.ara_free = None
        self.row_free = None
        self.fmid_free = None
        self.mkb_free = None
        self.scsb_free = None
        self.sctm_free = None
        self.msk_free = None
        self.hftr_free = [None, None]
        self.qke_free = None
        self.wd_slot = [None, None]
        self.ws_slot = [None, None]
        self._declare_dram()
        self._declare_consts()
        self._declare_sbuf()

    def op(self, eng, meth, *a, wait=(), inc=None, incv=1, **kw):
        waits = []
        for (s, v) in _flat(list(wait)):
            if self.sat[eng].get(s, -1) < v:
                self.sat[eng][s] = v
                waits.append((s.h, v))
        if inc is None:
            inc = {"tensor": self.sT, "vector": self.sV, "scalar": self.sA,
                   "gpsimd": self.sG}[eng]
        inc.n += incv
        ticket = (inc.h, incv)
        ret = (inc, inc.n)

        def f(e, meth=meth, a=a, kw=kw, waits=waits, ticket=ticket):
            for (h, v) in waits:
                e.wait_ge(h, v)
            getattr(e, meth)(*a, **kw).then_inc(*ticket)
        self.em[eng].append(f)
        return ret

    def waitv(self, eng, *tickets):
        for (s, v) in _flat(list(tickets)):
            if self.sat[eng].get(s, -1) < v:
                self.sat[eng][s] = v
                self.em[eng].append(lambda e, h=s.h, v=v: e.wait_ge(h, v))

    def dma(self, q, sem, out, in_, wait=(), slow=False):
        kw = {"allow_slow_non_contiguous": True} if slow else {}
        return self.op(q, "dma_start", wait=wait, inc=sem, incv=16,
                       out=out, in_=in_, **kw)

    def mm(self, bank_ids, meth, out, *a, wait=(), start=None, stop=None,
           first=False, last=False, **kw):
        w = list(wait)
        if first:
            for b_ in bank_ids:
                w.append(self.bank_r[b_])
        if meth == "matmul":
            t = self.op("tensor", "matmul", out, *a, wait=w,
                        start=start, stop=stop, **kw)
        else:
            t = self.op("tensor", meth, out, *a, wait=w, **kw)
        if last:
            for b_ in bank_ids:
                self.bank_w[b_] = t
        return t

    def bank_read(self, eng, meth, bank_ids, *a, wait=(), **kw):
        w = list(wait) + [self.bank_w[b_] for b_ in bank_ids]
        t = self.op(eng, meth, *a, wait=w, **kw)
        for b_ in bank_ids:
            self.bank_r[b_] = t
        return t

    def _declare_dram(self):
        nc = self.nc

        def inp(name, shape):
            return nc.dram_tensor(name, list(shape), F32, kind="ExternalInput").ap()
        self.h_fm = inp("h_fm", (D, T))
        self.h_tm = inp("h_tm", (T, D))
        self.hp_fm = inp("hp_fm", (D, T))
        self.hp_tm = inp("hp_tm", (T, D))
        self.uv = inp("uv", (D, 2 * R))
        self.projT = inp("projT", (D, D))
        self.w1h = inp("w1h", (MHC, 128, 8 * 128))
        self.w2w = inp("w2w", (MH, D))
        self.upb = inp("upb", (NE, FHC, 128, 8 * 256))
        self.dwn = inp("dwn", (NE, FH, D))
        self.cenT = inp("cenT", (D, NE))
        self.n1w = inp("n1w", (D,))
        self.n2w = inp("n2w", (D,))
        self.fsc = inp("fsc", (D,))
        self.pbv = inp("pbv", (D,))
        self.b1v = inp("b1v", (MH,))
        self.b2v = inp("b2v", (D,))
        self.dlog = inp("dlog", (1, R))
        self.alog = inp("alog", (1, R))
        self.glog = inp("glog", (1, 1))
        self.kker = inp("kker", (1, 4))
        self.w1bc = inp("w1bc", (128, D))
        self.y_out = nc.dram_tensor("y_out", [D, T], F32, kind="ExternalOutput").ap()

        def scr(name, shape):
            return nc.dram_tensor(name, list(shape), BF16, kind="Internal").ap()
        self.sc_proj = [scr("sc_proj_h", (NCH, 128, D)), scr("sc_proj_l", (NCH, 128, D))]
        self.sc_w1 = [scr("sc_w1_h", (MHC, 128, 8 * 128)),
                      scr("sc_w1_l", (MHC, 128, 8 * 128))]
        self.sc_w2 = [scr("sc_w2_h", (MHC, 128, D)), scr("sc_w2_l", (MHC, 128, D))]
        self.sc_up = [scr("sc_up_h", (NE, FHC, 128, 8 * 256)),
                      scr("sc_up_l", (NE, FHC, 128, 8 * 256))]
        self.sc_dw = [scr("sc_dw_h", (NE, FHC, 128, D)),
                      scr("sc_dw_l", (NE, FHC, 128, D))]

    def _declare_consts(self):
        nc = self.nc

        def ic(name, arr):
            return nc.inline_tensor(np.ascontiguousarray(arr), name=name).ap()
        self.IDT_I = ic("c_idt", np.eye(128, dtype=np.float32))
        self.ONESR_I = ic("c_onesr", np.ones((1, 128), np.float32))
        self.ONESC_I = ic("c_onesc", np.ones((128, 1), np.float32))
        self.TRIU_I = ic("c_triu", np.triu(np.ones((128, 128), np.float32)))
        dt_np = np.zeros((128, 4 * 128), np.float32)
        for tau in range(4):
            for j in range(128 - tau):
                dt_np[j, tau * 128 + j + tau] = 1.0
        self.DTAU_I = ic("c_dtau", dt_np)
        pt_np = np.zeros((3, 3 * 128), np.float32)
        for tau in range(1, 4):
            for s in range(3):
                i = tau + s - 3
                if 0 <= i < 128:
                    pt_np[s, (tau - 1) * 128 + i] = 1.0
        self.PTAU_I = ic("c_ptau", pt_np)
        self.GQ_I = ic("c_gq", np.broadcast_to(
            (np.arange(128) % 32).astype(np.float32), (16, 128)))
        self.GQ2_I = ic("c_gq2", np.broadcast_to(
            (np.arange(128) + 1).astype(np.float32), (16, 128)))
        self.GKB_I = ic("c_gkb", np.broadcast_to(
            (127 - np.arange(128)).astype(np.float32)[:, None], (128, 16)))
        be_np = np.zeros((8, 8 * 128), ml_dtypes.bfloat16)
        for e in range(8):
            be_np[e, e * 128:(e + 1) * 128] = 1.0
        self.BE_I = ic("c_be", be_np)

    def _declare_sbuf(self):
        nc = self.nc
        sb = nc.alloc_sbuf_tensor
        self.AR_A = sb("AR_A", [128, NCH * TT], F32).ap()
        self.ATN = sb("ATN", [128, 10240], F32).ap()
        self.AR_B = sb("AR_B", [128, 16384], BF16).ap()
        self.AR_W = sb("AR_W", [128, 16384], BF16).ap()
        self.STG = sb("STG", [128, 2048], F32).ap()
        self.HNB = sb("HNB", [128, 8192], BF16).ap()
        self.HFT = sb("HFT", [128, 1024], F32).ap()
        self.GEN = sb("GEN", [128, 6144], F32).ap()
        self.S_SB = sb("S_SB", [16, 1024], F32).ap()
        self.CST = sb("CST", [128, 2048], F32).ap()
        self.CSTB = sb("CSTB", [128, 2048], BF16).ap()
        self.XTR = sb("XTR", [128, 3072], F32).ap()
        self.pbk = [nc.alloc_psum_tensor("pb%d" % i, [128, 512], F32).ap()
                    for i in range(8)]
        C = self.CST
        self.C_IDT = C[:, 0:128]
        self.C_TRIU = C[:, 128:256]
        self.C_CONV = C[:, 256:384]
        self.C_W1BC = C[:, 384:1408]
        self.C_UVW = C[:, 1408:1664]
        self.C_CT = C[:, 1664:1728]
        self.N2W = C[:, 1728:1736]
        self.FSCW = C[:, 1736:1744]
        self.PB_ = C[:, 1744:1752]
        self.B2_ = C[:, 1752:1760]
        self.B1_ = C[:, 1760:1792]
        self.N1W = C[:, 1792:1800]
        self.C_BND = C[0:3, 1800:1928]
        self.C_FKB = C[:, 1928:1944]
        self.C_GKB = C[:, 1944:1960]
        self.BE_B = self.CSTB[0:8, 0:1024]
        self.MSK_B = self.CSTB[0:8, 1024:1536]
        self.MKB_B = self.CSTB[:, 1536:2048]
        G = self.GEN
        self.G_XNF = [G[:, 0:512], G[:, 512:1024]]
        self.G_TMP = [G[:, 1024:1536], G[:, 1536:2048]]
        self.F_Q0 = G[0:16, 2048:2176]
        self.F_KT = G[0:16, 2176:2304]
        self.F_Q2 = G[0:16, 2304:2432]
        self.F_SCR = G[0:16, 2432:2560]
        self.LNG_C = G[0:16, 2560:2561]
        self.ALP_C = G[0:16, 2561:2562]
        self.G32_C = G[0:16, 2562:2563]
        self.G128_C = G[0:16, 2563:2564]
        self.GROW = G[0:1, 2564:2580]
        self.LNGROW = G[0:1, 2580:2596]
        self.AROW = G[0:1, 2596:2612]
        self.GK = G[0:1, 2612:2616]
        self.GKB_B = G[:, 2616:2620]
        self.CSQ = G[0:8, 2620:2621]
        self.CSQ_R = G[0:1, 2624:2632]
        self.SIGG = G[0:1, 2632:2633]
        self.ONES_R = G[0:1, 2640:2768]
        self.ONES_C = G[:, 2768:2769]
        self.DLOG_S = G[0:1, 2769:2785]
        self.ALOG_S = G[0:1, 2785:2801]
        self.GLOG_S = G[0:1, 2801:2802]
        self.KKER_S = G[0:1, 2802:2806]
        self.B_HB = [G[0:3, 2816 + i * 1024:2816 + (i + 1) * 1024] for i in range(2)]
        self.B_HBST = G[0:3, 4864:4868]
        self.HBSQ = G[0:3, 4872:5896]
        X = self.XTR
        self.MROW = X[0:1, 0:512]
        self.VROW = X[0:1, 512:1024]
        self.IROW = X[0:1, 1024:1536]
        self.MIROW = X[0:1, 1536:2048]
        self.SQROW = X[0:1, 2048:2560]
        self.SC_SB = X[0:8, 2560:3072]
        self.SCTM = self.GEN[:, 5896:5904]
        self.EQT = self.GEN[:, 5904:5912]
        self.MNV = self.GEN[:, 5912:5913]
        self.NYT = self.GEN[:, 5913:5914]
        self.EXC = self.GEN[:, 5914:5922]
        Bf = self.ATN
        self.B_MIX = Bf[:, 0:4096]
        self.B_HTM = [Bf[:, 4096 + i * 1024:4096 + (i + 1) * 1024] for i in range(2)]
        self.B_HRAW = [Bf[:, 6144 + i * 1024:6144 + (i + 1) * 1024] for i in range(2)]
        self.B_QKT = [Bf[:, 8192 + i * 32:8192 + (i + 1) * 32] for i in range(2)]
        def ring(base, w, p=128):
            return [Bf[0:p, base + i * w:base + (i + 1) * w] for i in range(2)]
        self.B_SQ = ring(8256, 32)
        self.B_SSN = ring(8320, 8)
        self.B_QNT = ring(8336, 16)
        self.B_KNT = ring(8368, 16)
        self.B_KB = ring(8400, 16)
        self.B_INVT = ring(8432, 1)
        self.B_QNR = Bf[0:16, 8448:8576]
        self.B_KNR = Bf[0:16, 8576:8704]
        self.B_QT = [Bf[0:16, 8704 + i * 128:8704 + (i + 1) * 128]
                     for i in range(4)]
        self.B_KT_ = Bf[0:16, 9216:9344]
        self.B_Q2 = ring(9344, 128, 16)
        self.B_ASB = Bf[:, 9600:9728]
        self.B_QKE = Bf[0:32, 9728:10240]
        Bb = self.AR_B
        self.M_HI = Bb[:, 0:8192]
        self.M_LO = Bb[:, 8192:16384]
        self.FMID_HI = Bb[:, 0:8192]
        self.FMID_LO = Bb[:, 8192:16384]
        Wb = self.AR_W
        self.W_STG = [self.STG[:, 0:1024], self.STG[:, 1024:2048]]
        self.W_HI = [Wb[:, i * 1024:(i + 1) * 1024] for i in range(2)]
        self.W_LO = [Wb[:, 2048 + i * 1024:2048 + (i + 1) * 1024] for i in range(2)]
        self.WS_HI = [Wb[:, 4096 + i * 2048:4096 + (i + 1) * 2048] for i in range(2)]
        self.WS_LO = [Wb[:, 8192 + i * 2048:8192 + (i + 1) * 2048] for i in range(2)]
        self.WD_HI = [Wb[:, 12288 + i * 1024:12288 + (i + 1) * 1024] for i in range(2)]
        self.WD_LO = [Wb[:, 14336 + i * 1024:14336 + (i + 1) * 1024] for i in range(2)]
        self.HN_HI = self.HNB[:, 0:4096]
        self.HN_LO = self.HNB[:, 4096:8192]
        self.HS_HI = self.HNB[:, 0:2048]
        self.HS_LO = self.HNB[:, 2048:4096]
        self.XN_HI = self.HNB[:, 0:4096]
        self.XN_LO = self.HNB[:, 4096:8192]

    def tap(self, name, src_ap, shape, wait=()):
        if name not in self.tap_names or name in self.taps:
            return
        ap = self.nc.dram_tensor("tap_" + name, list(shape), F32,
                                 kind="ExternalOutput").ap()
        self.taps[name] = self.dma("scalar", self.dO, ap, src_ap, wait=wait)

    def p_setup(self):
        s = self
        d = lambda out, in_, w=(), slow=False: s.dma("sync", s.dC, out, in_, wait=w, slow=slow)
        t_idt = d(s.C_IDT, s.IDT_I)
        d(s.C_TRIU, s.TRIU_I)
        t_onesr = d(s.ONES_R, s.ONESR_I)
        t_onesc = d(s.ONES_C, s.ONESC_I)
        d(s.C_W1BC, s.w1bc)
        t_gkb = d(s.C_GKB, s.GKB_I)
        t_gq = d(s.G_TMP[0][0:16, 0:128], s.GQ_I)
        t_gq2 = d(s.G_TMP[0][0:16, 128:256], s.GQ2_I)
        d(s.BE_B, s.BE_I)
        t_dtau = d(s.B_MIX[:, 0:512], s.DTAU_I)
        t_ptau = d(s.B_MIX[0:3, 512:896], s.PTAU_I)
        t_uv = d(s.C_UVW, s.uv.rearrange("(c p) r -> p c r", p=128))
        t_ct = d(s.C_CT, s.cenT.rearrange("(c p) e -> p c e", p=128))
        t_n1 = d(s.N1W, s.n1w.rearrange("(c p) -> p c", p=128), slow=True)
        d(s.N2W, s.n2w.rearrange("(c p) -> p c", p=128), slow=True)
        d(s.FSCW, s.fsc.rearrange("(c p) -> p c", p=128), slow=True)
        d(s.PB_, s.pbv.rearrange("(c p) -> p c", p=128), slow=True)
        d(s.B2_, s.b2v.rearrange("(c p) -> p c", p=128), slow=True)
        d(s.B1_, s.b1v.rearrange("(c p) -> p c", p=128), slow=True)
        t_dl = d(s.DLOG_S, s.dlog)
        t_al = d(s.ALOG_S, s.alog)
        t_gl = d(s.GLOG_S, s.glog)
        t_kk = d(s.KKER_S, s.kker)
        L = (s.dC, s.dC.n)

        op = s.op
        a1 = op("scalar", "activation", out=s.GROW, in_=s.DLOG_S,
                func=AF.Sigmoid, wait=[L])
        v1 = op("vector", "tensor_scalar", out=s.GROW, in0=s.GROW,
                scalar1=0.15, scalar2=1.0, op0=ALU.max, op1=ALU.min, wait=[a1])
        a2 = op("scalar", "activation", out=s.LNGROW, in_=s.GROW,
                func=AF.Ln, wait=[v1])
        a3 = op("scalar", "activation", out=s.AROW, in_=s.ALOG_S,
                func=AF.Sigmoid, wait=[L])
        a4 = op("scalar", "activation", out=s.SIGG, in_=s.GLOG_S,
                func=AF.Sigmoid, wait=[L])
        v2 = op("vector", "tensor_single_scalar", s.GK, s.KKER_S, s.SIGG,
                op=ALU.mult, wait=[a4, L])
        m1 = s.mm([3], "transpose", s.pbk[3][0:16, 0:1], s.LNGROW,
                  s.C_IDT[0:1, 0:1], wait=[a2, L], first=True, last=True)
        v3 = s.bank_read("vector", "tensor_copy", [3], s.LNG_C,
                         s.pbk[3][0:16, 0:1])
        m2 = s.mm([3], "transpose", s.pbk[3][0:16, 0:1], s.AROW,
                  s.C_IDT[0:1, 0:1], wait=[a3], first=True, last=True)
        v4 = s.bank_read("vector", "tensor_copy", [3], s.ALP_C,
                         s.pbk[3][0:16, 0:1])
        v5 = op("vector", "tensor_single_scalar", s.F_SCR,
                s.G_TMP[0][0:16, 0:128], s.LNG_C, op=ALU.mult, wait=[L, v3])
        a5 = op("scalar", "activation", out=s.F_Q0, in_=s.F_SCR, func=AF.Exp,
                wait=[v5])
        op("vector", "tensor_single_scalar", s.F_Q0, s.F_Q0, s.ALP_C,
           op=ALU.mult, wait=[a5, v4])
        v7 = op("vector", "tensor_scalar", out=s.F_SCR, in0=s.F_SCR,
                scalar1=-1.0, scalar2=None, op0=ALU.mult, wait=[v5, a5])
        a6 = op("scalar", "activation", out=s.F_KT, in_=s.F_SCR, func=AF.Exp,
                wait=[v7])
        v8 = op("vector", "tensor_single_scalar", s.F_SCR,
                s.G_TMP[0][0:16, 128:256], s.LNG_C, op=ALU.mult,
                wait=[L, v3, a6])
        a7 = op("scalar", "activation", out=s.F_Q2, in_=s.F_SCR, func=AF.Exp,
                wait=[v8])
        op("vector", "tensor_single_scalar", s.F_Q2, s.F_Q2, s.ALP_C,
           op=ALU.mult, wait=[a7, v4])
        vA = op("vector", "tensor_scalar", out=s.G32_C, in0=s.LNG_C,
                scalar1=32.0, scalar2=None, op0=ALU.mult, wait=[v3])
        op("scalar", "activation", out=s.G32_C, in_=s.G32_C, func=AF.Exp,
           wait=[vA])
        vB = op("vector", "tensor_scalar", out=s.G128_C, in0=s.LNG_C,
                scalar1=128.0, scalar2=None, op0=ALU.mult, wait=[v3])
        op("scalar", "activation", out=s.G128_C, in_=s.G128_C, func=AF.Exp,
           wait=[vB])
        mB = s.mm([3], "matmul", s.pbk[3][:, 0:16], s.ONES_R, s.LNGROW,
                  start=True, stop=True, wait=[L, a2], first=True,
                  last=True)
        vC = s.bank_read("vector", "tensor_tensor", [3], s.C_FKB, s.C_GKB,
                         s.pbk[3][:, 0:16], op=ALU.mult, wait=[L])
        op("scalar", "activation", out=s.C_FKB, in_=s.C_FKB, func=AF.Exp,
           wait=[vC])
        mC = s.mm([3], "matmul", s.pbk[3][:, 0:4], s.ONES_R, s.GK,
                  start=True, stop=True, wait=[v2], first=True, last=True)
        vD = s.bank_read("vector", "tensor_copy", [3], s.GKB_B, s.pbk[3][:, 0:4])
        vE = op("vector", "tensor_single_scalar", s.C_CONV, s.B_MIX[:, 0:128],
                s.GKB_B[:, 0:1], op=ALU.mult, wait=[vD, L])
        for tau in range(1, 4):
            vE = op("vector", "scalar_tensor_tensor", s.C_CONV,
                    s.B_MIX[:, tau * 128:(tau + 1) * 128],
                    s.GKB_B[:, tau:tau + 1], s.C_CONV,
                    op0=ALU.mult, op1=ALU.add, wait=[vE])
        vF = op("vector", "tensor_single_scalar", s.C_BND,
                s.B_MIX[0:3, 512:640], s.GKB_B[0:3, 1:2], op=ALU.mult,
                wait=[vD, L])
        for tau in range(2, 4):
            vF = op("vector", "scalar_tensor_tensor", s.C_BND,
                    s.B_MIX[0:3, 512 + (tau - 1) * 128:512 + tau * 128],
                    s.GKB_B[0:3, tau:tau + 1], s.C_BND,
                    op0=ALU.mult, op1=ALU.add, wait=[vF])
        vU = None
        for c in range(NCH):
            vU = op("vector", "tensor_single_scalar",
                    s.C_UVW[:, c * 32:(c + 1) * 32],
                    s.C_UVW[:, c * 32:(c + 1) * 32], s.N1W[:, c:c + 1],
                    op=ALU.mult, wait=[L, vU])
        aS = op("scalar", "activation", out=s.G_TMP[1][:, 0:64], in_=s.C_CT,
                func=AF.Square, wait=[L])
        mS = s.mm([3], "matmul", s.pbk[3][0:1, 0:64], s.ONES_C,
                  s.G_TMP[1][:, 0:64], start=True, stop=True,
                  wait=[aS, L], first=True, last=True)
        vS = s.bank_read("vector", "tensor_reduce", [3], s.CSQ_R,
                         s.pbk[3][0:1, 0:64].rearrange("p (c e) -> p e c", c=8),
                         axis=AX.X, op=ALU.add)
        mS2 = s.mm([3], "transpose", s.pbk[3][0:8, 64:65], s.CSQ_R,
                   s.C_IDT[0:1, 0:1], wait=[vS], first=True, last=True)
        s.bank_read("vector", "tensor_copy", [3], s.CSQ, s.pbk[3][0:8, 64:65])
        op("vector", "tensor_scalar", out=s.C_CT, in0=s.C_CT,
           scalar1=-2.0, scalar2=None, op0=ALU.mult, wait=[L, mS])
        op("vector", "memset", s.S_SB, 0.0)
        self.setupV = (s.sV, s.sV.n)
        self.setupA = (s.sA, s.sA.n)
        self.setupT = (s.sT, s.sT.n)
        self.setupW = (s.dC, s.dC.n)

    def p_split(self):
        s = self
        blocks = []
        for c in range(NCH):
            blocks.append((("proj", c), s.projT[c * 128:(c + 1) * 128, :],
                           s.sc_proj[0][c], s.sc_proj[1][c]))
        for cp in range(MHC):
            blocks.append((("w1", cp), s.w1h[cp], s.sc_w1[0][cp], s.sc_w1[1][cp]))
        for c in range(MHC):
            blocks.append((("w2", c), s.w2w[c * 128:(c + 1) * 128, :],
                           s.sc_w2[0][c], s.sc_w2[1][c]))
        for e in range(NE):
            for cp in range(FHC):
                for hh in range(2):
                    sl = slice(hh * 1024, (hh + 1) * 1024)
                    blocks.append((("up", e, cp, hh), s.upb[e, cp][:, sl],
                                   s.sc_up[0][e, cp][:, sl],
                                   s.sc_up[1][e, cp][:, sl]))
            for c in range(FHC):
                blocks.append((("dw", e, c), s.dwn[e, c * 128:(c + 1) * 128, :],
                               s.sc_dw[0][e, c], s.sc_dw[1][e, c]))
        self.split_done = {}
        slot_free = [None, None]
        tins = [None] * len(blocks)
        for i in range(min(2, len(blocks))):
            tins[i] = s.dma("gpsimd", s.dAs[i % 2], s.W_STG[i % 2],
                            blocks[i][1], wait=[])
        wfree = [None, None]
        for i, (key, srcap, dsth, dstl) in enumerate(blocks):
            r = i % 2
            thi = s.op("gpsimd", "tensor_copy", s.W_HI[r], s.W_STG[r],
                       wait=[tins[i], wfree[r]])
            tlo = s.op("gpsimd", "tensor_sub", s.W_LO[r], s.W_STG[r],
                       s.W_HI[r], wait=[thi])
            s.dma("gpsimd", s.dGs[r], dsth, s.W_HI[r], wait=[thi])
            so2 = s.dma("gpsimd", s.dGs[r], dstl, s.W_LO[r], wait=[tlo])
            wfree[r] = so2
            self.split_done[key] = so2
            if i + 2 < len(blocks):
                tins[i + 2] = s.dma("gpsimd", s.dAs[r], s.W_STG[r],
                                    blocks[i + 2][1], wait=[tlo])

    def qkraw_tile(self, tau, fm_ap):
        s = self
        fmv = fm_ap.rearrange("(c p) t -> p c t", p=128)
        last = None
        for c in range(NCH):
            r = c % 2
            t_h = s.dma("sync", s.dHF[r], s.HFT[:, r * 512:(r + 1) * 512],
                        fmv[:, c, tau * TT:(tau + 1) * TT],
                        wait=[self.hftr_free[r]])
            last = s.mm([4], "matmul", s.pbk[4][0:32, :],
                        s.C_UVW[:, c * 32:(c + 1) * 32],
                        s.HFT[:, r * 512:(r + 1) * 512],
                        start=(c == 0), stop=(c == NCH - 1),
                        wait=[t_h, self.setupV] + (
                            [self.qke_free] if c == 0 else []),
                        first=(c == 0), last=(c == NCH - 1))
            self.hftr_free[r] = last
        t_qke = s.bank_read("vector", "tensor_copy", [4], s.B_QKE,
                            s.pbk[4][0:32, :])
        return last, t_qke

    def chain_stats(self, g, tm_ap, is_prefix, t_qke):
        s = self
        r = g % 2
        op = s.op
        st = self.ast
        ch = st["ch"][r]
        t_raw = s.dma("sync", s.dHR[r], s.B_HRAW[r],
                      tm_ap[g * 128:(g + 1) * 128, :],
                      wait=[st["hraw_free"][r]])
        a_sq = op("scalar", "activation", out=s.B_HTM[r], in_=s.B_HRAW[r],
                  func=AF.Square, accum_out=s.B_SSN[r][:, 0:1],
                  wait=[t_raw, st["htm_free"][r], st["ssn_free"][r]])
        v_ms = op("vector", "tensor_scalar", out=s.B_SSN[r][:, 1:2],
                  in0=s.B_SSN[r][:, 0:1], scalar1=1.0 / D, scalar2=EPS_RMS,
                  op0=ALU.mult, op1=ALU.add, wait=[a_sq])
        a_sr = op("scalar", "activation", out=s.B_SSN[r][:, 2:3],
                  in_=s.B_SSN[r][:, 1:2], func=AF.Sqrt, wait=[v_ms])
        v_iv = op("vector", "reciprocal", s.B_INVT[r], s.B_SSN[r][:, 2:3],
                  wait=[a_sr, st["invt_free"][r]])
        v_h1 = op("vector", "tensor_single_scalar", s.B_HTM[r], s.B_HRAW[r],
                  s.B_INVT[r], op=ALU.mult, wait=[v_iv, a_sq])
        v_h2 = op("vector", "tensor_tensor", s.B_HTM[r], s.B_HTM[r],
                  s.C_W1BC, op=ALU.mult, wait=[v_h1])
        st["hraw_free"][r] = v_h2
        m_tr = s.mm([3], "transpose", s.pbk[3][:, 0:32],
                    s.B_QKE[0:32, (g % 4) * 128:(g % 4) * 128 + 128],
                    s.C_IDT[0:32, 0:32], wait=[t_qke, st["qkt_free"][r]],
                    first=True, last=True)
        v_qs = s.bank_read("vector", "tensor_single_scalar", [3], s.B_QKT[r],
                           s.pbk[3][:, 0:32], s.B_INVT[r], op=ALU.mult,
                           wait=[v_iv])
        st["invt_free"][r] = v_qs
        a_q2 = op("scalar", "activation", out=s.B_SQ[r], in_=s.B_QKT[r],
                  func=AF.Square, wait=[v_qs, st["sq_free"][r]])
        v_r1 = op("vector", "reduce_sum", s.B_SSN[r][:, 4:5],
                  s.B_SQ[r][:, 0:16], axis=AX.X, wait=[a_q2])
        v_r2 = op("vector", "reduce_sum", s.B_SSN[r][:, 5:6],
                  s.B_SQ[r][:, 16:32], axis=AX.X, wait=[a_q2])
        st["sq_free"][r] = v_r2
        a_n = op("scalar", "activation", out=s.B_SSN[r][:, 6:8],
                 in_=s.B_SSN[r][:, 4:6], func=AF.Sqrt, wait=[v_r1, v_r2])
        v_mx = op("vector", "tensor_scalar", out=s.B_SSN[r][:, 6:8],
                  in0=s.B_SSN[r][:, 6:8], scalar1=EPS_L2, scalar2=None,
                  op0=ALU.max, wait=[a_n])
        v_rc = op("vector", "reciprocal", s.B_SSN[r][:, 6:8],
                  s.B_SSN[r][:, 6:8], wait=[v_mx])
        v_kn = op("vector", "tensor_single_scalar", s.B_KNT[r],
                  s.B_QKT[r][:, 16:32], s.B_SSN[r][:, 7:8], op=ALU.mult,
                  wait=[v_rc, st["knt_free"][r]])
        st["qkt_free"][r] = v_kn
        st["ssn_free"][r] = v_kn
        v_kb = op("vector", "tensor_tensor", s.B_KB[r], s.B_KNT[r], s.C_FKB,
                  op=ALU.mult, wait=[v_kn, self.setupA, st["kb_free"][r]])
        st["knt_free"][r] = v_kb
        ch["v_h2"] = v_h2
        ch["v_kn"] = v_kn
        ch["v_kb"] = v_kb
        if not is_prefix:
            v_qn = op("vector", "tensor_single_scalar", s.B_QNT[r],
                      s.B_QKT[r][:, 0:16], s.B_SSN[r][:, 6:7], op=ALU.mult,
                      wait=[v_rc, st["qnt_free"][r]])
            st["qkt_free"][r] = v_qn
            st["ssn_free"][r] = v_qn
            ch["v_qn"] = v_qn

    def chain_tail(self, g):
        s = self
        r = g % 2
        op = s.op
        st = self.ast
        ch = st["ch"][r]
        m_t1 = s.mm([5], "transpose", s.pbk[5][0:16, 0:128], s.B_QNT[r],
                    s.C_IDT[:, 0:128], wait=[ch["v_qn"]], first=True,
                    last=True)
        v_t1 = s.bank_read("vector", "tensor_copy", [5], s.B_QNR,
                           s.pbk[5][0:16, 0:128], wait=[st["qnr_free"]])
        m_t2 = s.mm([5], "transpose", s.pbk[5][0:16, 0:128], s.B_KNT[r],
                    s.C_IDT[:, 0:128], wait=[ch["v_kn"]], first=True,
                    last=True)
        v_t2 = s.bank_read("vector", "tensor_copy", [5], s.B_KNR,
                           s.pbk[5][0:16, 0:128], wait=[st["knr_free"]])
        st["qnt_free"][r] = m_t1
        st["knt_free"][r] = [ch["v_kb"], m_t2]
        v_q0 = op("vector", "tensor_tensor", s.B_QT[0], s.B_QNR, s.F_Q0,
                  op=ALU.mult, wait=[v_t1, st["qt_free"], self.setupA])
        v_prev = v_q0
        for dd in range(1, 4):
            v_prev = op("vector", "tensor_single_scalar", s.B_QT[dd],
                        s.B_QT[dd - 1], s.G32_C, op=ALU.mult,
                        wait=[v_prev])
        v_kt = op("vector", "tensor_tensor", s.B_KT_, s.B_KNR, s.F_KT,
                  op=ALU.mult, wait=[v_t2, st["kt_free"]])
        v_q2 = op("vector", "tensor_tensor", s.B_Q2[r], s.B_QNR, s.F_Q2,
                  op=ALU.mult, wait=[v_t1, st["q2_free"][r]])
        st["qnr_free"] = v_q2
        st["knr_free"] = v_kt
        ch["v_prev"] = v_prev
        ch["v_kt"] = v_kt

    def body_A(self, g):
        s = self
        r = g % 2
        op = s.op
        st = self.ast
        ch = st["ch"][r]
        lastA = None
        for bj in range(4):
            for bi in range(4):
                dd = max(bi - bj, 0)
                w = [ch["v_prev"], ch["v_kt"]] if (bi == 0 and bj == 0) \
                    else []
                lastA = s.mm([0], "matmul",
                             s.pbk[0][bj * 32:(bj + 1) * 32,
                                      bi * 32:bi * 32 + 32],
                             s.B_KT_[:, bj * 32:(bj + 1) * 32],
                             s.B_QT[dd][:, bi * 32:(bi + 1) * 32],
                             start=(bi == 0), stop=(bi == 3),
                             tile_position=(0, 32 * bj),
                             wait=w, first=(bi == 0 and bj == 0),
                             last=(bi == 3 and bj == 3))
        v_am = s.bank_read("vector", "tensor_tensor", [0], s.B_ASB,
                           s.pbk[0][:, 0:128], s.C_TRIU, op=ALU.mult,
                           wait=[st["asb_free"]])
        v_ac = op("vector", "tensor_tensor", s.B_ASB, s.B_ASB, s.C_CONV,
                  op=ALU.add, wait=[v_am])
        st["qt_free"] = v_am
        st["kt_free"] = v_am
        ch["v_ac"] = v_ac

    def body_decay(self, g, half):
        s = self
        r = g % 2
        st = self.ast
        ch = st["ch"][r]
        bank = s.pbk[1 + half]
        lastd = None
        for dq in range(4):
            c = half * 4 + dq
            dsl = slice(c * 128, (c + 1) * 128)
            osl = slice(dq * 128, (dq + 1) * 128)
            w0 = [ch["v_ac"], ch["v_h2"], st["hb"][(g + 1) % 2],
                  st["s_upd"]] if dq == 0 else []
            s.mm([1 + half], "matmul", bank[:, osl],
                 s.B_HTM[r][:, dsl], s.B_ASB,
                 start=(dq == 0), stop=False, wait=w0,
                 first=(dq == 0))
            s.mm([1 + half], "matmul", bank[:, osl],
                 s.B_HB[(g + 1) % 2][0:3, dsl], s.C_BND,
                 start=False, stop=False)
            lastd = s.mm([1 + half], "matmul", bank[:, osl],
                         s.S_SB[0:16, dsl], s.B_Q2[r],
                         start=False, stop=(dq == 3),
                         last=(dq == 3))
        return lastd

    def body_evac(self, g):
        s = self
        st = self.ast
        evs = []
        for half in range(2):
            dst = s.B_MIX.rearrange("p (c t) -> p c t", c=NCH)[
                :, half * 4:(half + 1) * 4,
                (g % 4) * 128:(g % 4) * 128 + 128]
            e_ = s.bank_read("vector", "tensor_copy", [1 + half], dst,
                             s.pbk[1 + half].rearrange(
                                 "p (c t) -> p c t", c=4),
                             wait=[st["mix_free"]])
            evs.append(e_)
        return evs[1]

    def hb_dma(self, g, tm_ap):
        s = self
        r = g % 2
        return s.dma("sync", s.dHB[r], s.B_HB[r],
                     tm_ap[g * 128 + 125:g * 128 + 128, :],
                     wait=[s.ast["hb_cons"][r]])

    def body_state(self, g, is_prefix, tm_ap, t_hb=None):
        s = self
        r = g % 2
        op = s.op
        st = self.ast
        ch = st["ch"][r]
        m_s0 = s.mm([6], "matmul", s.pbk[6][0:16, :], s.B_KB[r],
                    s.B_HTM[r][:, 0:512], start=True, stop=True,
                    wait=[ch["v_kb"], ch["v_h2"]], first=True, last=True)
        m_s1 = s.mm([7], "matmul", s.pbk[7][0:16, :], s.B_KB[r],
                    s.B_HTM[r][:, 512:1024], start=True, stop=True,
                    first=True, last=True)
        st["htm_free"][r] = m_s1
        st["kb_free"][r] = m_s1
        v_s1 = op("vector", "tensor_single_scalar", s.S_SB, s.S_SB, s.G128_C,
                  op=ALU.mult, wait=[m_s1, st["s_cons"], self.setupA])
        v_s2 = s.bank_read("vector", "tensor_tensor", [6], s.S_SB[:, 0:512],
                           s.S_SB[:, 0:512], s.pbk[6][0:16, :], op=ALU.add,
                           wait=[v_s1])
        v_s3 = s.bank_read("vector", "tensor_tensor", [7], s.S_SB[:, 512:1024],
                           s.S_SB[:, 512:1024], s.pbk[7][0:16, :], op=ALU.add,
                           wait=[v_s2])
        st["s_upd"] = v_s3
        st["s_cons"] = v_s3
        if (not is_prefix) or g == NG - 1:
            if t_hb is None:
                t_hb = s.hb_dma(g, tm_ap)
            a_hs = op("scalar", "activation", out=s.HBSQ, in_=s.B_HB[r],
                      func=AF.Square, accum_out=s.B_HBST[0:3, 0:1],
                      wait=[t_hb])
            v_hm = op("vector", "tensor_scalar", out=s.B_HBST[0:3, 1:2],
                      in0=s.B_HBST[0:3, 0:1], scalar1=1.0 / D,
                      scalar2=EPS_RMS, op0=ALU.mult, op1=ALU.add, wait=[a_hs])
            a_h2 = op("scalar", "activation", out=s.B_HBST[0:3, 2:3],
                      in_=s.B_HBST[0:3, 1:2], func=AF.Sqrt, wait=[v_hm])
            v_hr = op("vector", "reciprocal", s.B_HBST[0:3, 3:4],
                      s.B_HBST[0:3, 2:3], wait=[a_h2])
            v_hb1 = op("vector", "tensor_single_scalar", s.B_HB[r], s.B_HB[r],
                       s.B_HBST[0:3, 3:4], op=ALU.mult, wait=[v_hr])
            v_hb2 = op("vector", "tensor_tensor", s.B_HB[r], s.B_HB[r],
                       s.C_W1BC[0:3, :], op=ALU.mult, wait=[v_hb1])
            st["hb"][r] = v_hb2

    def init_attn_state(self):
        empty_ch = lambda: {"v_h2": None, "v_kn": None, "v_kb": None,
                            "v_qn": None, "v_prev": None, "v_kt": None,
                            "v_ac": None}
        self.ast = {
            "hraw_free": [None, None], "htm_free": [None, None],
            "qkt_free": [None, None], "hb": [None, None],
            "hb_cons": [None, None], "s_upd": None, "s_cons": None,
            "ssn_free": [None, None], "invt_free": [None, None],
            "qnt_free": [None, None], "knt_free": [None, None],
            "kb_free": [None, None], "sq_free": [None, None],
            "q2_free": [None, None],
            "qnr_free": None, "knr_free": None, "qt_free": None,
            "kt_free": None, "asb_free": None, "mix_free": None,
            "ch": [empty_ch(), empty_ch()],
        }

    def p_prefix(self):
        s = self
        last_mm, t_qke = s.qkraw_tile(0, s.hp_fm)
        s.chain_stats(0, s.hp_tm, True, t_qke)
        for g in range(NG):
            nx = g + 1
            if nx < NG:
                if nx % 4 == 0:
                    self.qke_free = (s.sT, s.sT.n)
                    last_mm, t_qke = s.qkraw_tile(nx // 4, s.hp_fm)
                s.chain_stats(nx, s.hp_tm, True, t_qke)
            s.body_state(g, True, s.hp_tm)
        self.qke_free = (s.sT, s.sT.n)

    def proj_tile(self, tau, mix_ev):
        s = self
        op = s.op
        fmv = s.h_fm.rearrange("(c p) t -> p c t", p=128)
        sp = []
        pend = None
        for c in range(NCH):
            src = s.B_MIX[:, c * 512:(c + 1) * 512]
            a_h = op("scalar", "copy", out=s.HN_HI[:, c * 512:(c + 1) * 512],
                     in_=src, wait=[mix_ev, self.hn_free])
            if pend is not None:
                psrc, pc, pa = pend
                sp.append(op("vector", "tensor_sub",
                             s.HN_LO[:, pc * 512:(pc + 1) * 512], psrc,
                             s.HN_HI[:, pc * 512:(pc + 1) * 512], wait=[pa]))
            pend = (src, c, a_h)
        psrc, pc, pa = pend
        sp.append(op("vector", "tensor_sub",
                     s.HN_LO[:, pc * 512:(pc + 1) * 512], psrc,
                     s.HN_HI[:, pc * 512:(pc + 1) * 512], wait=[pa]))
        self.ast["mix_free"] = [sp[-1], (s.sA, s.sA.n)]
        wslot = self.wd_slot
        for kk in range(NCH):
            r = kk % 2
            th = s.dma("sync", s.dWD[r], s.WD_HI[r], s.sc_proj[0][kk],
                       wait=[wslot[r], self.split_done[("proj", kk)]])
            tl = s.dma("sync", s.dWD[r], s.WD_LO[r], s.sc_proj[1][kk])
            used = None
            for dout in range(NCH):
                lh = s.WD_HI[r][:, dout * 128:(dout + 1) * 128]
                ll = s.WD_LO[r][:, dout * 128:(dout + 1) * 128]
                xh = s.HN_HI[:, kk * 512:(kk + 1) * 512]
                xl = s.HN_LO[:, kk * 512:(kk + 1) * 512]
                w = [tl, sp[kk]] if dout == 0 else []
                s.mm([dout], "matmul", s.pbk[dout], lh, xh,
                     start=(kk == 0), stop=False, wait=w, first=(kk == 0))
                s.mm([dout], "matmul", s.pbk[dout], lh, xl,
                     start=False, stop=False)
                used = s.mm([dout], "matmul", s.pbk[dout], ll, xh,
                            start=False, stop=(kk == NCH - 1),
                            last=(kk == NCH - 1))
            wslot[r] = used
        self.hn_free = used
        h2t = []
        pend = None

        def h2_add(dout, t_h, a_e):
            r2 = dout % 2
            v_a = op("vector", "tensor_add",
                     s.AR_A[:, dout * 512:(dout + 1) * 512],
                     s.G_TMP[dout % 2],
                     s.HFT[:, r2 * 512:(r2 + 1) * 512],
                     wait=[a_e, t_h, self.ara_free])
            self.gtmp_free[dout % 2] = v_a
            self.hftr_free[r2] = v_a
            h2t.append(v_a)

        for dout in range(NCH):
            r2 = dout % 2
            t_h = s.dma("sync", s.dHF[r2], s.HFT[:, r2 * 512:(r2 + 1) * 512],
                        fmv[:, dout, tau * TT:(tau + 1) * TT],
                        wait=[self.hftr_free[r2]])
            a_e = s.bank_read("scalar", "activation", [dout],
                              out=s.G_TMP[dout % 2], in_=s.pbk[dout],
                              func=AF.Identity, bias=s.PB_[:, dout:dout + 1],
                              wait=[self.gtmp_free[dout % 2], self.setupW])
            if pend is not None:
                h2_add(*pend)
            pend = (dout, t_h, a_e)
        h2_add(*pend)
        return h2t

    def rms_stats(self, srcs, eps):
        s = self
        op = s.op
        lastm = None
        for c, (src, w) in enumerate(srcs):
            ww = list(w) + [self.gtmp_free[c % 2]]
            if c % 2 == 0:
                a_sq = op("scalar", "activation", out=s.G_TMP[c % 2],
                          in_=src, func=AF.Square, wait=ww)
            else:
                a_sq = op("vector", "tensor_tensor", s.G_TMP[c % 2],
                          src, src, op=ALU.mult, wait=ww)
            lastm = s.mm([5], "matmul", s.pbk[5][0:1, :], s.ONES_C,
                         s.G_TMP[c % 2], start=(c == 0), stop=(c == 7),
                         wait=[a_sq], first=(c == 0), last=(c == 7))
            self.gtmp_free[c % 2] = lastm
        v_m = s.bank_read("vector", "tensor_scalar", [5], out=s.MROW,
                          in0=s.pbk[5][0:1, :], scalar1=1.0 / D, scalar2=eps,
                          op0=ALU.mult, op1=ALU.add, wait=[self.row_free])
        a_s = op("scalar", "activation", out=s.SQROW, in_=s.MROW,
                 func=AF.Sqrt, wait=[v_m])
        v_i = op("vector", "reciprocal", s.IROW, s.SQROW, wait=[a_s])
        m_b = s.mm([2], "matmul", s.pbk[2], s.ONES_R, s.IROW,
                   start=True, stop=True, wait=[v_i], first=True, last=True)
        self.row_free = m_b
        return m_b

    def ln_tile(self, tau, ticks_in):
        s = self
        op = s.op
        for c in range(NCH):
            src = s.AR_A[:, c * 512:(c + 1) * 512]
            w = [ticks_in[c]] if ticks_in else []
            s.mm([6], "matmul", s.pbk[6][0:1, :], s.ONES_C, src,
                 start=(c == 0), stop=(c == 7), wait=w,
                 first=(c == 0), last=(c == 7))
            ww = list(w) + [self.gtmp_free[c % 2]]
            if c % 2 == 0:
                a_sq = op("scalar", "activation", out=s.G_TMP[c % 2],
                          in_=src, func=AF.Square, wait=ww)
            else:
                a_sq = op("vector", "tensor_tensor", s.G_TMP[c % 2],
                          src, src, op=ALU.mult, wait=ww)
            lastq = s.mm([3], "matmul", s.pbk[3][0:1, :], s.ONES_C,
                         s.G_TMP[c % 2], start=(c == 0), stop=(c == 7),
                         wait=[a_sq], first=(c == 0), last=(c == 7))
            self.gtmp_free[c % 2] = lastq
        v_m = s.bank_read("vector", "tensor_scalar", [6], out=s.MROW,
                          in0=s.pbk[6][0:1, :], scalar1=1.0 / D, scalar2=None,
                          op0=ALU.mult, wait=[self.row_free])
        v_v = s.bank_read("vector", "tensor_scalar", [3], out=s.VROW,
                          in0=s.pbk[3][0:1, :], scalar1=1.0 / D, scalar2=None,
                          op0=ALU.mult)
        a_m2 = op("scalar", "activation", out=s.SQROW, in_=s.MROW,
                  func=AF.Square, wait=[v_m])
        v_v2 = op("vector", "tensor_sub", s.VROW, s.VROW, s.SQROW,
                  wait=[v_v, a_m2])
        v_v3 = op("vector", "tensor_scalar", out=s.VROW, in0=s.VROW,
                  scalar1=1.0, scalar2=EPS_LN, op0=ALU.mult, op1=ALU.add,
                  wait=[v_v2])
        a_sd = op("scalar", "activation", out=s.SQROW, in_=s.VROW,
                  func=AF.Sqrt, wait=[v_v3])
        v_i = op("vector", "reciprocal", s.IROW, s.SQROW, wait=[a_sd])
        v_mi = op("vector", "tensor_mul", s.MIROW, s.MROW, s.IROW, wait=[v_i])
        m_b1 = s.mm([2], "matmul", s.pbk[2], s.ONES_R, s.IROW,
                    start=True, stop=True, wait=[v_i], first=True, last=True)
        m_b2 = s.mm([4], "matmul", s.pbk[4], s.ONES_R, s.MIROW,
                    start=True, stop=True, wait=[v_mi, self.qke_free],
                    first=True, last=True)
        self.row_free = m_b2
        outt = []
        for c in range(NCH):
            src = s.AR_A[:, c * 512:(c + 1) * 512]
            v_1 = s.bank_read("vector", "tensor_tensor", [2],
                              s.G_TMP[c % 2], src, s.pbk[2], op=ALU.mult,
                              wait=[self.gtmp_free[c % 2]])
            v_2 = s.bank_read("vector", "tensor_sub", [4], src,
                              s.G_TMP[c % 2], s.pbk[4], wait=[v_1])
            self.gtmp_free[c % 2] = v_2
            outt.append(v_2)
        return outt

    def mlp_tile(self, tau, h2t):
        s = self
        op = s.op
        s.rms_stats([(s.AR_A[:, c * 512:(c + 1) * 512], [h2t[c]])
                     for c in range(NCH)], EPS_RMS)
        fmv = s.h_fm.rearrange("(c p) t -> p c t", p=128)
        zt = [None] * NCH
        for sub in range(2):
            ssl = slice(sub * 256, sub * 256 + 256)
            hnt = []
            pend = None
            for c in range(NCH):
                csl = slice(c * 512 + sub * 256, c * 512 + sub * 256 + 256)
                hsl = slice(c * 256, (c + 1) * 256)
                v_x = s.bank_read("vector", "scalar_tensor_tensor", [2],
                                  s.G_TMP[c % 2][:, 0:256],
                                  s.AR_A[:, csl],
                                  s.N2W[:, c:c + 1], s.pbk[2][:, ssl],
                                  op0=ALU.mult, op1=ALU.mult,
                                  wait=[self.gtmp_free[c % 2], self.hn_free])
                a_h = op("scalar", "copy", out=s.HS_HI[:, hsl],
                         in_=s.G_TMP[c % 2][:, 0:256], wait=[v_x])
                if pend is not None:
                    pc, phsl, pa = pend
                    v_l = op("vector", "tensor_sub", s.HS_LO[:, phsl],
                             s.G_TMP[pc % 2][:, 0:256], s.HS_HI[:, phsl],
                             wait=[pa])
                    self.gtmp_free[pc % 2] = v_l
                    hnt.append(v_l)
                pend = (c, hsl, a_h)
            pc, phsl, pa = pend
            v_l = op("vector", "tensor_sub", s.HS_LO[:, phsl],
                     s.G_TMP[pc % 2][:, 0:256], s.HS_HI[:, phsl],
                     wait=[pa])
            self.gtmp_free[pc % 2] = v_l
            hnt.append(v_l)
            wslot2 = self.ws_slot
            slot4 = [wslot2[0], wslot2[1], wslot2[0], wslot2[1]]
            sems4 = [s.dWS[0], s.dWS[1], s.dW2[0], s.dW2[1]]
            w2pre = [None] * 4
            for k2 in range(2):
                rq = k2 % 2
                w2pre[k2] = (
                    s.dma("sync", sems4[k2], s.WS_HI[rq][:, 0:1024],
                          s.sc_w2[0][k2],
                          wait=[slot4[k2], self.split_done[("w2", k2)]]),
                    s.dma("sync", sems4[k2], s.WS_LO[rq][:, 0:1024],
                          s.sc_w2[1][k2]))
            wslot = self.wd_slot
            midt = []
            mpend = None
            for cp in range(MHC):
                r = cp % 2
                th = s.dma("sync", s.dWD[r], s.WD_HI[r], s.sc_w1[0][cp],
                           wait=[wslot[r], self.split_done[("w1", cp)]])
                tl = s.dma("sync", s.dWD[r], s.WD_LO[r], s.sc_w1[1][cp])
                bank = cp % 2
                lastw = None
                for kk in range(NCH):
                    lh = s.WD_HI[r][:, kk * 128:(kk + 1) * 128]
                    ll = s.WD_LO[r][:, kk * 128:(kk + 1) * 128]
                    xh = s.HS_HI[:, kk * 256:(kk + 1) * 256]
                    xl = s.HS_LO[:, kk * 256:(kk + 1) * 256]
                    w = ([tl, self.mid_free] if kk == 0 else []) + [hnt[kk]]
                    s.mm([bank], "matmul", s.pbk[bank][:, 0:256], lh, xh,
                         start=(kk == 0), stop=False, wait=w, first=(kk == 0))
                    s.mm([bank], "matmul", s.pbk[bank][:, 0:256], lh, xl,
                         start=False, stop=False)
                    lastw = s.mm([bank], "matmul", s.pbk[bank][:, 0:256],
                                 ll, xh, start=False, stop=(kk == NCH - 1),
                                 last=(kk == NCH - 1))
                wslot[r] = lastw
                msl = slice(cp * 256, (cp + 1) * 256)
                a_g = s.bank_read("scalar", "activation", [bank],
                                  out=s.G_TMP[cp % 2][:, 0:256],
                                  in_=s.pbk[bank][:, 0:256],
                                  func=AF.Gelu_apprx_tanh,
                                  bias=s.B1_[:, cp:cp + 1],
                                  wait=[self.gtmp_free[cp % 2]])
                a_h = op("scalar", "copy", out=s.M_HI[:, msl],
                         in_=s.G_TMP[cp % 2][:, 0:256], wait=[a_g])
                if mpend is not None:
                    pcp, pmsl, pa = mpend
                    v_l = op("vector", "tensor_sub", s.M_LO[:, pmsl],
                             s.G_TMP[pcp % 2][:, 0:256], s.M_HI[:, pmsl],
                             wait=[pa])
                    self.gtmp_free[pcp % 2] = v_l
                    midt.append(v_l)
                mpend = (cp, msl, a_h)
                if cp == 1:
                    for k2x in (2, 3):
                        rqx = k2x % 2
                        offx = (k2x // 2) * 1024
                        w2pre[k2x] = (
                            s.dma("sync", sems4[k2x],
                                  s.WS_HI[rqx][:, offx:offx + 1024],
                                  s.sc_w2[0][k2x],
                                  wait=[slot4[k2x],
                                        self.split_done[("w2", k2x)]]),
                            s.dma("sync", sems4[k2x],
                                  s.WS_LO[rqx][:, offx:offx + 1024],
                                  s.sc_w2[1][k2x]))
            pcp, pmsl, pa = mpend
            v_l = op("vector", "tensor_sub", s.M_LO[:, pmsl],
                     s.G_TMP[pcp % 2][:, 0:256], s.M_HI[:, pmsl],
                     wait=[pa])
            self.gtmp_free[pcp % 2] = v_l
            midt.append(v_l)
            self.hn_free = wslot[(MHC - 1) % 2]
            used = None
            for k2 in range(MHC):
                idx = k2 % 4
                r = idx % 2
                off = (idx // 2) * 1024
                if k2 < 4:
                    th, tl = w2pre[idx]
                else:
                    th = s.dma("sync", sems4[idx],
                               s.WS_HI[r][:, off:off + 1024],
                               s.sc_w2[0][k2],
                               wait=[slot4[idx],
                                     self.split_done[("w2", k2)]])
                    tl = s.dma("sync", sems4[idx],
                               s.WS_LO[r][:, off:off + 1024],
                               s.sc_w2[1][k2])
                for dout in range(NCH):
                    lh = s.WS_HI[r][:, off + dout * 128:
                                    off + (dout + 1) * 128]
                    ll = s.WS_LO[r][:, off + dout * 128:
                                    off + (dout + 1) * 128]
                    xh = s.M_HI[:, k2 * 256:(k2 + 1) * 256]
                    xl = s.M_LO[:, k2 * 256:(k2 + 1) * 256]
                    w = [tl, midt[k2]] if dout == 0 else []
                    s.mm([dout], "matmul", s.pbk[dout][:, ssl], lh, xh,
                         start=(k2 == 0), stop=False, wait=w,
                         first=(k2 == 0))
                    s.mm([dout], "matmul", s.pbk[dout][:, ssl], lh, xl,
                         start=False, stop=False)
                    used = s.mm([dout], "matmul", s.pbk[dout][:, ssl],
                                ll, xh, start=False, stop=(k2 == MHC - 1),
                                last=(k2 == MHC - 1))
                slot4[idx] = used
            wslot2[0] = [slot4[0], slot4[2]]
            wslot2[1] = [slot4[1], slot4[3]]
            self.mid_free = used
            pend = None

            def z_add(dout, t_h, a_e):
                csl = slice(dout * 512 + sub * 256,
                            dout * 512 + sub * 256 + 256)
                r2 = dout % 2
                v_1 = op("vector", "tensor_add", s.G_TMP[dout % 2][:, 0:256],
                         s.G_TMP[dout % 2][:, 0:256],
                         s.HFT[:, r2 * 512 + sub * 256:
                               r2 * 512 + sub * 256 + 256],
                         wait=[a_e, t_h])
                v_2 = op("vector", "tensor_add", s.AR_A[:, csl],
                         s.AR_A[:, csl], s.G_TMP[dout % 2][:, 0:256],
                         wait=[v_1])
                self.gtmp_free[dout % 2] = v_2
                self.hftr_free[r2] = v_1
                zt[dout] = v_2

            for dout in range(NCH):
                r2 = dout % 2
                t_h = s.dma("sync", s.dHF[r2],
                            s.HFT[:, r2 * 512 + sub * 256:
                                  r2 * 512 + sub * 256 + 256],
                            fmv[:, dout, tau * TT + sub * 256:
                                tau * TT + sub * 256 + 256],
                            wait=[self.hftr_free[r2]])
                a_e = s.bank_read("scalar", "activation", [dout],
                                  out=s.G_TMP[dout % 2][:, 0:256],
                                  in_=s.pbk[dout][:, ssl],
                                  func=AF.Identity,
                                  bias=s.B2_[:, dout:dout + 1],
                                  wait=[self.gtmp_free[dout % 2]])
                if pend is not None:
                    z_add(*pend)
                pend = (dout, t_h, a_e)
            z_add(*pend)
        return zt

    def ffn_tile(self, tau, yt):
        s = self
        op = s.op
        m_b = s.rms_stats([(s.AR_A[:, c * 512:(c + 1) * 512], [yt[c]])
                           for c in range(NCH)], EPS_RMS)
        lastsc = None
        xnt = []
        pend = None
        for c in range(NCH):
            rr = c % 2
            v_x = s.bank_read("vector", "scalar_tensor_tensor", [2],
                              s.G_XNF[rr],
                              s.AR_A[:, c * 512:(c + 1) * 512],
                              s.FSCW[:, c:c + 1], s.pbk[2],
                              op0=ALU.mult, op1=ALU.mult,
                              wait=[self.xnf_free[rr], self.hn_free])
            a_h = op("scalar", "copy",
                     out=s.XN_HI[:, c * 512:(c + 1) * 512],
                     in_=s.G_XNF[rr], wait=[v_x])
            lastsc = s.mm([5], "matmul", s.pbk[5][0:8, :],
                          s.C_CT[:, c * 8:(c + 1) * 8], s.G_XNF[rr],
                          start=(c == 0), stop=(c == 7),
                          wait=[v_x] + ([self.setupT] if c == 0 else []),
                          first=(c == 0), last=(c == 7))
            if pend is not None:
                pc, pa, psc = pend
                v_l = op("vector", "tensor_sub",
                         s.XN_LO[:, pc * 512:(pc + 1) * 512],
                         s.G_XNF[pc % 2],
                         s.XN_HI[:, pc * 512:(pc + 1) * 512], wait=[pa])
                self.xnf_free[pc % 2] = [psc, v_l]
                xnt.append(v_l)
            pend = (c, a_h, lastsc)
        pc, pa, psc = pend
        v_l = op("vector", "tensor_sub",
                 s.XN_LO[:, pc * 512:(pc + 1) * 512], s.G_XNF[pc % 2],
                 s.XN_HI[:, pc * 512:(pc + 1) * 512], wait=[pa])
        self.xnf_free[pc % 2] = [psc, v_l]
        xnt.append(v_l)
        a_sc = s.bank_read("scalar", "activation", [5], out=s.SC_SB,
                           in_=s.pbk[5][0:8, :], func=AF.Identity,
                           bias=s.CSQ, wait=[self.scsb_free])
        mskt = []
        for sub in range(4):
            m_t = s.mm([3], "transpose", s.pbk[3][:, 0:8],
                       s.SC_SB[0:8, sub * 128:(sub + 1) * 128],
                       s.C_IDT[0:8, 0:8], wait=[a_sc], first=True, last=True)
            v_c = s.bank_read("vector", "tensor_copy", [3], s.SCTM,
                              s.pbk[3][:, 0:8], wait=[self.sctm_free])
            v_mn = op("vector", "tensor_reduce", s.MNV, s.SCTM, axis=AX.X,
                      op=ALU.min, wait=[v_c])
            v_eq = op("vector", "tensor_single_scalar", s.EQT, s.SCTM, s.MNV,
                      op=ALU.is_equal, wait=[v_mn])
            v_pr = op("vector", "memset", s.NYT, 1.0, wait=[v_eq])
            for e in range(NE):
                v_ex = op("vector", "tensor_mul", s.EXC[:, e:e + 1],
                          s.EQT[:, e:e + 1], s.NYT, wait=[v_pr])
                v_pr = op("vector", "tensor_sub", s.NYT, s.NYT,
                          s.EXC[:, e:e + 1], wait=[v_ex])
            self.sctm_free = v_pr
            m_tb = s.mm([3], "transpose", s.pbk[3][0:8, 0:128], s.EXC,
                        s.C_IDT[:, 0:128], wait=[v_pr], first=True,
                        last=True)
            v_mk = s.bank_read("vector", "tensor_copy", [3],
                               s.MSK_B[0:8, sub * 128:(sub + 1) * 128],
                               s.pbk[3][0:8, 0:128], wait=[self.msk_free])
            mskt.append(v_mk)
        self.scsb_free = m_tb
        z2t = list(yt)
        first_e = [True] * NCH
        last_mb = None
        uppre = [None, None]
        for e in range(NE):
            for cpq in range(2):
                rq = cpq % 2
                uppre[rq] = (
                    s.dma("sync", s.dWS[rq], s.WS_HI[rq], s.sc_up[0][e, cpq],
                          wait=[self.ws_slot[rq],
                                self.split_done[("up", e, cpq, 0)],
                                self.split_done[("up", e, cpq, 1)]]),
                    s.dma("sync", s.dWS[rq], s.WS_LO[rq], s.sc_up[1][e, cpq]))
            m_mb = s.mm([5], "matmul", s.pbk[5],
                        s.BE_B[0:8, e * 128:(e + 1) * 128],
                        s.MSK_B[0:8, 0:512], start=True, stop=True,
                        wait=[mskt[-1], self.setupW], first=True, last=True)
            last_mb = m_mb
            v_mb = s.bank_read("vector", "tensor_copy", [5], s.MKB_B,
                               s.pbk[5], wait=[self.mkb_free])
            wslotd = self.wd_slot
            dwpre = [None, None]
            for k2q in range(2):
                rq = k2q % 2
                dwpre[rq] = (
                    s.dma("sync", s.dWD[rq], s.WD_HI[rq], s.sc_dw[0][e, k2q],
                          wait=[wslotd[rq], self.split_done[("dw", e, k2q)]]),
                    s.dma("sync", s.dWD[rq], s.WD_LO[rq], s.sc_dw[1][e, k2q]))
            wslot = self.ws_slot
            fmidt = []
            for cp in range(FHC):
                r = cp % 2
                if cp < 2:
                    th, tl = uppre[r]
                else:
                    th = s.dma("sync", s.dWS[r], s.WS_HI[r], s.sc_up[0][e, cp],
                               wait=[wslot[r],
                                     self.split_done[("up", e, cp, 0)],
                                     self.split_done[("up", e, cp, 1)]])
                    tl = s.dma("sync", s.dWS[r], s.WS_LO[r], s.sc_up[1][e, cp])
                pa, pg = (cp % 2) * 2, (cp % 2) * 2 + 1
                lastu = None
                for half, bank in ((0, pa), (1, pg)):
                    for kk in range(NCH):
                        lh = s.WS_HI[r][:, kk * 256 + half * 128:
                                        kk * 256 + half * 128 + 128]
                        ll = s.WS_LO[r][:, kk * 256 + half * 128:
                                        kk * 256 + half * 128 + 128]
                        xh = s.XN_HI[:, kk * 512:(kk + 1) * 512]
                        xl = s.XN_LO[:, kk * 512:(kk + 1) * 512]
                        w = [tl, xnt[-1]] if kk == 0 else []
                        s.mm([bank], "matmul", s.pbk[bank], lh, xh,
                             start=(kk == 0), stop=False, wait=w,
                             first=(kk == 0))
                        s.mm([bank], "matmul", s.pbk[bank], lh, xl,
                             start=False, stop=False)
                        lastu = s.mm([bank], "matmul", s.pbk[bank], ll, xh,
                                     start=False, stop=(kk == NCH - 1),
                                     last=(kk == NCH - 1))
                wslot[r] = lastu
                a_si = s.bank_read("scalar", "activation", [pg],
                                   out=s.G_TMP[cp % 2], in_=s.pbk[pg],
                                   func=AF.Silu,
                                   wait=[self.gtmp_free[cp % 2]])
                v_mu = s.bank_read("vector", "tensor_tensor", [pa],
                                   s.G_XNF[cp % 2], s.pbk[pa],
                                   s.G_TMP[cp % 2], op=ALU.mult,
                                   wait=[a_si, self.xnf_free[cp % 2],
                                         self.fmid_free])
                v_mu = op("vector", "tensor_mul", s.G_XNF[cp % 2],
                          s.G_XNF[cp % 2], s.MKB_B, wait=[v_mu, v_mb])
                self.mkb_free = v_mu
                a_mh = op("scalar", "copy",
                          out=s.FMID_HI[:, cp * 512:(cp + 1) * 512],
                          in_=s.G_XNF[cp % 2], wait=[v_mu])
                v_ml = op("vector", "tensor_sub",
                          s.FMID_LO[:, cp * 512:(cp + 1) * 512],
                          s.G_XNF[cp % 2],
                          s.FMID_HI[:, cp * 512:(cp + 1) * 512],
                          wait=[a_mh])
                self.gtmp_free[cp % 2] = v_mu
                self.xnf_free[cp % 2] = [v_ml, (s.sA, s.sA.n)]
                fmidt.append(v_ml)
            self.hn_free = wslot[(FHC - 1) % 2]
            wslot = self.wd_slot
            used = None
            for k2 in range(FHC):
                r = k2 % 2
                if k2 < 2:
                    th, tl = dwpre[r]
                else:
                    th = s.dma("sync", s.dWD[r], s.WD_HI[r], s.sc_dw[0][e, k2],
                               wait=[wslot[r], self.split_done[("dw", e, k2)]])
                    tl = s.dma("sync", s.dWD[r], s.WD_LO[r], s.sc_dw[1][e, k2])
                for dout in range(NCH):
                    lh = s.WD_HI[r][:, dout * 128:(dout + 1) * 128]
                    ll = s.WD_LO[r][:, dout * 128:(dout + 1) * 128]
                    xh = s.FMID_HI[:, k2 * 512:(k2 + 1) * 512]
                    xl = s.FMID_LO[:, k2 * 512:(k2 + 1) * 512]
                    w = [tl, fmidt[k2]] if dout == 0 else []
                    s.mm([dout], "matmul", s.pbk[dout], lh, xh,
                         start=(k2 == 0), stop=False, wait=w,
                         first=(k2 == 0))
                    s.mm([dout], "matmul", s.pbk[dout], lh, xl,
                         start=False, stop=False)
                    used = s.mm([dout], "matmul", s.pbk[dout], ll, xh,
                                start=False, stop=(k2 == FHC - 1),
                                last=(k2 == FHC - 1))
                wslot[r] = used
            self.fmid_free = used
            nz = []
            for dout in range(NCH):
                if first_e[dout]:
                    v_z = s.bank_read("vector", "scalar_tensor_tensor",
                                      [dout],
                                      s.AR_A[:, dout * 512:(dout + 1) * 512],
                                      s.AR_A[:, dout * 512:(dout + 1) * 512],
                                      2.0, s.pbk[dout],
                                      op0=ALU.mult, op1=ALU.add,
                                      wait=[z2t[dout]])
                    first_e[dout] = False
                else:
                    v_z = s.bank_read("vector", "tensor_add", [dout],
                                      s.AR_A[:, dout * 512:(dout + 1) * 512],
                                      s.AR_A[:, dout * 512:(dout + 1) * 512],
                                      s.pbk[dout], wait=[z2t[dout]])
                nz.append(v_z)
            z2t = nz
        self.msk_free = last_mb
        return z2t

    def p_main(self):
        s = self
        st = self.ast
        last_mm, t_qke = s.qkraw_tile(0, s.h_fm)
        s.chain_stats(0, s.h_tm, False, t_qke)
        s.chain_tail(0)
        for tau in range(NTT):
            mix_ev = None
            for g in range(tau * 4, tau * 4 + 4):
                s.body_A(g)
                t_hb = s.hb_dma(g, s.h_tm)
                nx = g + 1
                if nx < NG:
                    if nx % 4 == 0:
                        self.qke_free = (s.sT, s.sT.n)
                        last_mm, t_qke = s.qkraw_tile(nx // 4, s.h_fm)
                    s.chain_stats(nx, s.h_tm, False, t_qke)
                s.body_decay(g, 0)
                if nx < NG:
                    s.chain_tail(nx)
                ld1 = s.body_decay(g, 1)
                st["q2_free"][g % 2] = ld1
                st["asb_free"] = ld1
                st["hb_cons"][(g + 1) % 2] = ld1
                mix_ev = s.body_evac(g)
                s.body_state(g, False, s.h_tm, t_hb)
            self.qke_free = (s.sT, s.sT.n)
            h2t = s.proj_tile(tau, mix_ev)
            s.tap("h2_%d" % tau, s.AR_A, (128, NCH * TT), wait=h2t)
            zt = s.mlp_tile(tau, h2t)
            s.tap("z_%d" % tau, s.AR_A, (128, NCH * TT), wait=zt)
            yt = s.ln_tile(tau, zt)
            s.tap("y_%d" % tau, s.AR_A, (128, NCH * TT), wait=yt)
            z2t = s.ffn_tile(tau, yt)
            ot = s.ln_tile(tau, z2t)
            s.tap("o_%d" % tau, s.AR_A, (128, NCH * TT), wait=ot)
            sts = []
            for c in range(NCH):
                sts.append(s.dma("scalar", s.dO,
                                 s.y_out[c * 128:(c + 1) * 128,
                                         tau * TT:(tau + 1) * TT],
                                 s.AR_A[:, c * 512:(c + 1) * 512],
                                 wait=[ot[c]]))
            self.ara_free = sts[-1]
        self.waitv("scalar", (s.dO, s.dO.n))
        self.waitv("gpsimd", (s.dGs[0], s.dGs[0].n), (s.dGs[1], s.dGs[1].n))

    def build(self):
        self.p_setup()
        self.p_split()
        self.init_attn_state()
        self.p_prefix()
        self.p_main()
        self.emit()
        return self.nc

    def emit(self):
        with self.nc.Block() as block:
            for name in self.ENGS:
                getattr(block, name)(
                    (lambda fns: lambda e: [f(e) for f in fns])(self.em[name]))


def _host_inputs(inputs):
    x = np.ascontiguousarray(np.asarray(inputs["L0_attn_h"], dtype=np.float32))
    u = np.asarray(inputs["L0_attn_u"], dtype=np.float32)
    v = np.asarray(inputs["L0_attn_v"], dtype=np.float32)
    uvc = np.ascontiguousarray(np.concatenate([u, v], axis=1))
    projT = np.ascontiguousarray(np.asarray(inputs["L0_attn_proj_w"],
                                            dtype=np.float32).T)
    w1 = np.asarray(inputs["L0_attn_mlp_w1"], dtype=np.float32)
    w1h = np.ascontiguousarray(
        w1.reshape(8, 128, MHC, 128).transpose(2, 1, 0, 3)
        .reshape(MHC, 128, 8 * 128))
    w2w = np.ascontiguousarray(np.asarray(inputs["L0_attn_mlp_w2"],
                                          dtype=np.float32))
    up = np.asarray(inputs["L0_ffn_up_w"], dtype=np.float32)
    upb = np.ascontiguousarray(
        up.reshape(NE, 8, 128, 2, FHC, 128)
        .transpose(0, 4, 2, 1, 3, 5)
        .reshape(NE, FHC, 128, 8 * 256))
    dwn = np.ascontiguousarray(np.asarray(inputs["L0_ffn_down_w"],
                                          dtype=np.float32))
    cenT = np.ascontiguousarray(np.asarray(inputs["L0_ffn_centroids"],
                                           dtype=np.float32).T)
    w1bc = np.ascontiguousarray(np.broadcast_to(
        np.asarray(inputs["L0_attn_norm1_w"], dtype=np.float32)[None, :],
        (128, D)))
    shared = {
        "uv": uvc, "projT": projT, "w1h": w1h, "w2w": w2w, "upb": upb,
        "dwn": dwn, "cenT": cenT,
        "n1w": np.ascontiguousarray(np.asarray(inputs["L0_attn_norm1_w"], np.float32)),
        "n2w": np.ascontiguousarray(np.asarray(inputs["L0_attn_norm2_w"], np.float32)),
        "fsc": np.ascontiguousarray(np.asarray(inputs["L0_ffn_scale"], np.float32)),
        "pbv": np.ascontiguousarray(np.asarray(inputs["L0_attn_proj_b"], np.float32)),
        "b1v": np.ascontiguousarray(np.asarray(inputs["L0_attn_mlp_b1"], np.float32)),
        "b2v": np.ascontiguousarray(np.asarray(inputs["L0_attn_mlp_b2"], np.float32)),
        "dlog": np.ascontiguousarray(
            np.asarray(inputs["L0_attn_decay_logit"], np.float32)[None, :]),
        "alog": np.ascontiguousarray(
            np.asarray(inputs["L0_attn_alpha_logit"], np.float32)[None, :]),
        "glog": np.ascontiguousarray(
            np.asarray(inputs["L0_attn_k_base_gate_logit"],
                       np.float32).reshape(1, 1)),
        "kker": np.ascontiguousarray(
            np.asarray(inputs["L0_attn_k_base_kernel"], np.float32)[None, :]),
        "w1bc": w1bc,
    }
    zeros_tm = np.zeros((T, D), np.float32)
    zeros_fm = np.zeros((D, T), np.float32)
    in_maps = []
    for core in range(8):
        b, half = core // 2, core % 2
        tok = x[b, half * T:(half + 1) * T]
        m = dict(shared)
        m["h_tm"] = np.ascontiguousarray(tok)
        m["h_fm"] = np.ascontiguousarray(tok.T)
        if half == 1:
            pre = x[b, 0:T]
            m["hp_tm"] = np.ascontiguousarray(pre)
            m["hp_fm"] = np.ascontiguousarray(pre.T)
        else:
            m["hp_tm"] = zeros_tm
            m["hp_fm"] = zeros_fm
        in_maps.append(m)
    return in_maps


def _get_nc():
    key = tuple(DEBUG_TAPS)
    if key not in _NC_CACHE:
        b = B(taps=DEBUG_TAPS)
        _NC_CACHE[key] = (b.build(), b)
    return _NC_CACHE[key]


def kernel(**inputs):
    nc, _ = _get_nc()
    in_maps = _host_inputs(inputs)
    res = run_bass_kernel_spmd(nc, in_maps, core_ids=list(range(8)))
    out = np.empty((4, 2 * T, D), np.float32)
    for core in range(8):
        b, half = core // 2, core % 2
        out[b, half * T:(half + 1) * T] = res.results[core]["y_out"].T
    return out
